# version 09
# baseline (speedup 1.0000x reference)
import numpy as np
import concourse.bass as bass
import concourse.mybir as mybir
from concourse.bass_utils import run_bass_kernel_spmd
from concourse.alu_op_type import AluOpType as OP

F32 = mybir.dt.float32
AF = mybir.ActivationFunctionType

B, L, D, ED, NST, KCV, RDT = 2, 8192, 1024, 2048, 16, 4, 64
TLOC = L // 4
ESL = ED // 8
BT = B * L
DT = D // 128
E2 = ESL // 128
TCA = 512
TCP = 1024
NCH = BT // TCP
TCF = 256
EPS = 1e-5

ENGS = ("y", "g", "v", "s", "t")


class Op:
    __slots__ = ("eng", "emit", "deps", "dma", "need_inc", "val", "acoll")

    def __init__(self, eng, emit, deps, dma, acoll=False):
        self.eng, self.emit, self.deps, self.dma = eng, emit, list(deps), dma
        self.acoll = acoll
        self.need_inc = dma or acoll
        self.val = None


class Plan:
    def __init__(self):
        self.ops = []
        self.last = {e: None for e in ENGS}
        self.bdeps = {e: [] for e in ENGS}

    def op(self, eng, emit, deps=(), dma=False, acoll=False):
        o = Op(eng, emit, [d for d in deps if d is not None], dma, acoll)
        if self.bdeps[eng]:
            o.deps.extend(self.bdeps[eng])
            self.bdeps[eng] = []
        self.ops.append(o)
        self.last[eng] = o
        return o

    def barrier(self):
        snap = [o for o in self.last.values() if o is not None]
        for e in ENGS:
            self.bdeps[e] = list(snap)

    def finalize(self):
        for o in self.ops:
            for d in o.deps:
                if d.eng != o.eng or d.dma or d.acoll:
                    d.need_inc = True
        ctr = {e: 0 for e in ENGS}
        for o in self.ops:
            if o.need_inc:
                ctr[o.eng] += 16 if o.dma else 1
                o.val = ctr[o.eng]
        self.totals = ctr


class PB:

    def __init__(self, ap):
        self.ap = ap
        self.readers = []
        self.writer = None


def build_nc():
    nc = bass.Bass("TRN2", target_bir_lowering=False)

    x_in = nc.dram_tensor("x_dmaj", [D, TLOC], F32, kind="ExternalInput").ap()
    id_in = nc.dram_tensor("ident_in", [128, 128], F32, kind="ExternalInput").ap()
    sel_in = nc.dram_tensor("sel_in", [128, 2 * NST * 128], F32, kind="ExternalInput").ap()
    out_d = nc.dram_tensor("out_dmaj", [D, TLOC], F32, kind="ExternalOutput").ap()
    lwin = []
    for i in range(2):
        lwin.append({k: nc.dram_tensor(f"l{i}_{k}", shp, F32, kind="ExternalInput").ap()
                     for k, shp in [("inwT", [D, 2 * ESL]), ("outwT", [ESL, D]),
                                    ("xpwT", [ESL, 96]), ("dtwT", [RDT, ESL]),
                                    ("convw", [128, E2 * KCV]), ("convb", [128, E2]),
                                    ("dtb", [128, E2]), ("alog", [128, E2 * NST]),
                                    ("dp", [128, E2]), ("normw", [128, DT])]})
    fwin = {nm: nc.dram_tensor(nm, shp, F32, kind="ExternalInput").ap()
            for nm, shp in [("cnw1", [D, D]), ("cnb1", [D, 1]),
                            ("cnw2", [D, D // 2]), ("cnb2", [D // 2, 1]),
                            ("cnw3", [D // 2, D // 4]), ("cnb3", [D // 4, 1]),
                            ("cnw4", [D // 4, 1]), ("cnb4", [1, 1]),
                            ("gcw1", [D, D // 2]), ("gcb1", [D // 2, 1]),
                            ("gcw2", [D // 2, D // 4]), ("gcb2", [D // 4, 1]),
                            ("gcw3", [D // 4, 1]), ("gcb3", [1, 1]), ("ar", [1, 1])]}
    xn_loc = nc.dram_tensor("xn_loc", [D, TLOC], F32, kind="Internal").ap()
    xn_all = nc.dram_tensor("xn_all", [8 * D, TLOC], F32, kind="Internal").ap()
    u_st = nc.dram_tensor("u_st", [ESL, BT], F32, kind="Internal").ap()
    zg_st = nc.dram_tensor("zg_st", [ESL, BT], F32, kind="Internal").ap()
    dbc_i = nc.dram_tensor("dbc_i", [96, BT], F32, kind="Internal").ap()
    dbc_o = nc.dram_tensor("dbc_o", [96, BT], F32, kind="Internal").ap()
    op_i = nc.dram_tensor("op_i", [8 * D, TLOC], F32, kind="Internal").ap()
    op_o = nc.dram_tensor("op_o", [D, TLOC], F32, kind="Internal").ap()
    gf_i = nc.dram_tensor("gf_i", [128, DT], F32, kind="Internal").ap()
    gf_o = nc.dram_tensor("gf_o", [128, DT], F32, kind="Internal").ap()
    xstash = nc.dram_tensor("xstash", [D, TLOC], F32, kind="Internal").ap()

    from contextlib import ExitStack
    with ExitStack() as stk:
        ec = stk.enter_context
        resid = ec(nc.sbuf_tensor("resid", [128, DT, TLOC], F32))
        smalls = ec(nc.sbuf_tensor("smalls", [128, 2048], F32))
        consts = ec(nc.sbuf_tensor("consts", [128, 512], F32))
        ident = ec(nc.sbuf_tensor("ident", [128, 128], F32))
        ones_t = ec(nc.sbuf_tensor("ones_t", [128, 128], F32))
        arena = ec(nc.sbuf_tensor("arena", [128, 29696], F32))
        ps_w = PB(ec(nc.psum_tensor("ps_w", [128, 512], F32)))
        ps_w2 = PB(ec(nc.psum_tensor("ps_w2", [128, 512], F32)))
        ps_a = PB(ec(nc.psum_tensor("ps_a", [128, 512], F32)))
        ps_b = PB(ec(nc.psum_tensor("ps_b", [128, 512], F32)))
        ps_y = PB(ec(nc.psum_tensor("ps_y", [128, 1024], F32)))
        sems = {e: ec(nc.semaphore(f"sem_{e}")) for e in ENGS}
        block = ec(nc.Block())

        p = Plan()

        c_off = [0]

        def calloc(n):
            o = c_off[0]
            c_off[0] += n
            assert c_off[0] <= 256
            return consts[:, o:o + n]

        pA = calloc(E2 * NST).rearrange("p (e n) -> p e n", e=E2)
        pA_raw = calloc(E2 * NST)
        dtb_raw = calloc(E2)
        convw_c = calloc(E2 * KCV).rearrange("p (e k) -> p e k", e=E2)
        convb_c = calloc(E2)
        dp_c = calloc(E2)
        normw_c = calloc(DT)
        ones_col = calloc(1)
        zeros_col = calloc(1)
        ones_row = ones_t[0:1, :]
        k0t = calloc(1)
        k1t = calloc(1)
        art = calloc(1)
        gft = calloc(1)
        _cmap = {}

        def constf(v):
            if v not in _cmap:
                _cmap[v] = calloc(1)
                _cinit.append(vmemset(_cmap[v], float(v)))
            return _cmap[v]
        _cinit = []

        class Ar:
            def __init__(self):
                self.off = 0

            def al(self, *shape):
                n = int(np.prod(shape))
                o = self.off
                self.off += n
                assert self.off <= 29696, f"arena overflow {self.off}"
                v = arena[:, o:o + n]
                if len(shape) == 2:
                    v = v.rearrange("p (a b) -> p a b", a=shape[0])
                return v

            def reset(self, off=0):
                self.off = off

        A = Ar()

        def dma(q, out, in_, deps=()):
            return p.op(q, lambda e, o=out, i=in_: e.dma_start(out=o, in_=i),
                        deps=deps, dma=True)

        def mm(bank, pairs, out_ap=None, deps=(), start=True, stop=True):
            tgt = out_ap if out_ap is not None else bank.ap[:]
            ops = []
            nmm = len(pairs)
            for j, (lh, rh) in enumerate(pairs):
                d = list(deps) + (bank.readers if (j == 0 and start) else []) \
                    + ([bank.writer] if (j == 0 and not start and bank.writer) else [])
                st = start and j == 0
                sp = stop and j == nmm - 1
                ops.append(p.op("t", lambda e, lh=lh, rh=rh, st=st, sp=sp, t=tgt:
                                e.matmul(t, lh, rh, start=st, stop=sp), deps=d))
            if start:
                bank.readers = []
            bank.writer = ops[-1]
            return ops[-1]

        def rd(bank, op):
            bank.readers.append(op)
            return op

        def act(out, in_, func, deps=(), src_bank=None, **kw):
            d = list(deps) + ([src_bank.writer] if src_bank else [])
            o = p.op("s", lambda e, o=out, i=in_, f=func, k=dict(kw):
                     e.activation(out=o, in_=i, func=f, **k), deps=d)
            if src_bank:
                rd(src_bank, o)
            return o

        def vtt(out, a, b, op_, deps=(), banks=()):
            d = list(deps) + [bk.writer for bk in banks if bk.writer]
            o = p.op("v", lambda e, o=out, x=a, y2=b, z=op_:
                     e.tensor_tensor(o, x, y2, z), deps=d)
            for bk in banks:
                rd(bk, o)
            return o

        def vts(out, a, s1, op_, deps=(), banks=()):
            d = list(deps) + [bk.writer for bk in banks if bk.writer]
            o = p.op("v", lambda e, o=out, x=a, s=s1, z=op_:
                     e.tensor_scalar(o, x, s, None, z), deps=d)
            for bk in banks:
                rd(bk, o)
            return o

        def vstt(out, in0, scal, in1, op0, op1, deps=(), banks=(), accum_out=None):
            d = list(deps) + [bk.writer for bk in banks if bk.writer]

            def f(e, o=out, i0=in0, sc=scal, i1=in1, a=op0, b2=op1, ac=accum_out):
                if ac is not None:
                    return e.scalar_tensor_tensor(o, i0, sc, i1, a, b2, accum_out=ac)
                return e.scalar_tensor_tensor(o, i0, sc, i1, a, b2)
            o = p.op("v", f, deps=d)
            for bk in banks:
                rd(bk, o)
            return o

        def vscan(out, d0, d1, init, deps=()):
            return p.op("v", lambda e, o=out, a=d0, b2=d1, i=init:
                        e.tensor_tensor_scan(o, a, b2, i, OP.mult, OP.add), deps=deps)

        def vcopy(out, in_, deps=()):
            return p.op("v", lambda e, o=out, i=in_: e.tensor_copy(o, i), deps=deps)

        def vred(out, in_, deps=()):
            return p.op("v", lambda e, o=out, i=in_:
                        e.tensor_reduce(out=o, in_=i, axis=mybir.AxisListType.X,
                                        op=OP.add), deps=deps)

        def vmemset(ap, val, deps=()):
            return p.op("v", lambda e, a=ap, v=val: e.memset(a, v), deps=deps)

        def coll(kind, op_, groups, in_ap, out_ap, deps=()):
            return p.op("g", lambda e, k=kind, o=op_, g=groups, i=in_ap, u2=out_ap:
                        e.collective_compute(k, o, replica_groups=g,
                                             ins=[i.opt()], outs=[u2.opt()]),
                        deps=deps, acoll=True)

        for dt in range(DT):
            dma("y", resid[:, dt, :], x_in[dt * 128:(dt + 1) * 128, :])
        dma("y", ident[:, :], id_in)
        vmemset(ones_col, 1.0)
        vmemset(ones_t[:, :], 1.0)
        vmemset(zeros_col, 0.0)
        dma("y", art[0:1, :], fwin["ar"])
        for _v in (EPS, -0.5, 1.0 / D):
            constf(_v)
        p.barrier()

        def ln_chunk(t0, tc, sqbuf, t1buf, out_buf, gf_ck=None, gfparts=None,
                     extra_dep_per_dt=None):
            sq_ops = [act(sqbuf[:, dt, 0:tc], resid[:, dt, t0:t0 + tc], AF.Square)
                      for dt in range(DT)]
            w_sx = mm(ps_w, [(ones_col, resid[:, dt, t0:t0 + tc]) for dt in range(DT)],
                      out_ap=ps_w.ap[0:1, 0:tc])
            w_sx2 = mm(ps_w2, [(ones_col, sqbuf[:, dt, 0:tc]) for dt in range(DT)],
                       out_ap=ps_w2.ap[0:1, 0:tc], deps=sq_ops)
            mu = smalls[0:1, 0:tc]
            ex2 = smalls[0:1, 512:512 + tc]
            var = smalls[0:1, 1024:1024 + tc]
            rstd = smalls[0:1, 1536:1536 + tc]
            e1 = vts(mu, ps_w.ap[0:1, 0:tc], 1.0 / D, OP.mult, banks=[ps_w])
            e2_ = vts(ex2, ps_w2.ap[0:1, 0:tc], 1.0 / D, OP.mult, banks=[ps_w2])
            e3 = vtt(var, mu, mu, OP.mult, deps=[e1])
            e4 = vtt(var, ex2, var, OP.subtract, deps=[e2_, e3])
            e5a = act(rstd, var, AF.Ln, bias=constf(EPS)[0:1, 0:1], deps=[e4])
            e5 = act(rstd, rstd, AF.Exp, scale=constf(-0.5)[0:1, 0:1], deps=[e5a])
            r1 = mm(ps_a, [(ones_row, mu)], out_ap=ps_a.ap[:, 0:tc],
                    deps=[e3])
            r2 = mm(ps_b, [(ones_row, rstd)], out_ap=ps_b.ap[:, 0:tc],
                    deps=[e5])
            outs = []
            for dt in range(DT):
                xd = [extra_dep_per_dt[dt]] if extra_dep_per_dt else []
                s1 = vtt(t1buf[:, dt, 0:tc], resid[:, dt, t0:t0 + tc], ps_a.ap[:, 0:tc],
                         OP.subtract, deps=[sq_ops[dt]] + xd, banks=[ps_a])
                ac = gfparts[:, dt, gf_ck:gf_ck + 1] if gf_ck is not None else None
                outs.append(vstt(out_buf[:, dt, 0:tc], t1buf[:, dt, 0:tc], 1.0,
                                 ps_b.ap[:, 0:tc], OP.bypass, OP.mult,
                                 deps=[s1], banks=[ps_b], accum_out=ac))
            return outs

        def ffn_phase():
            A.reset(0)
            w1 = A.al(DT, D)
            w2 = A.al(DT, D // 2)
            w3 = A.al(4, D // 4)
            w4 = A.al(2, 1)
            g1w = A.al(DT, D // 2)
            g2w = A.al(4, D // 4)
            g3w = A.al(2, 1)
            b1 = A.al(DT)
            b2 = A.al(4)
            b3 = A.al(2)
            b4 = A.al(1)
            gb1 = A.al(4)
            gb2 = A.al(2)
            gb3 = A.al(1)
            lnb = A.al(DT, TCF)
            h1 = A.al(DT, TCF)
            h2 = A.al(4, TCF)
            h3 = A.al(2, TCF)
            tmp8 = A.al(DT, TCF)
            t1b = A.al(DT, TCF)
            gfparts = A.al(DT, DT)
            gfp = A.al(DT)

            for src, dst in [(fwin["cnw1"], w1), (fwin["cnw2"], w2), (fwin["cnw3"], w3),
                             (fwin["cnw4"], w4), (fwin["gcw1"], g1w), (fwin["gcw2"], g2w),
                             (fwin["gcw3"], g3w)]:
                for k in range(src.shape[0] // 128):
                    dma("y", dst[:, k, :], src[k * 128:(k + 1) * 128, :])
            for src, dst in [(fwin["cnb1"], b1), (fwin["cnb2"], b2), (fwin["cnb3"], b3),
                             (fwin["gcb1"], gb1), (fwin["gcb2"], gb2)]:
                for k in range(src.shape[0] // 128):
                    dma("y", dst[:, k:k + 1], src[k * 128:(k + 1) * 128, :])
            dma("y", b4[0:1, 0:1], fwin["cnb4"])
            dma("y", gb3[0:1, 0:1], fwin["gcb3"])
            p.barrier()

            nchf = TLOC // TCF
            for ck in range(nchf):
                ln_chunk(ck * TCF, TCF, tmp8, t1b, lnb, gf_ck=ck, gfparts=gfparts)
                p.barrier()
            rg = vred(gfp[:, 0:DT], gfparts)
            d1 = dma("g", gf_i, gfp[:, 0:DT], deps=[rg])
            cc1 = coll("AllReduce", OP.add, [[0, 1, 2, 3], [4, 5, 6, 7]],
                       gf_i, gf_o, deps=[d1])
            d2 = dma("y", gfp[:, 0:DT], gf_o, deps=[cc1])
            mlv = vts(gfp[:, 0:DT], gfp[:, 0:DT], 1.0 / L, OP.mult, deps=[d2])

            gh1 = smalls[0:128, 0:4]
            gh2 = smalls[0:128, 4:6]
            lastb1 = []
            for m in range(4):
                wop = mm(ps_w, [(g1w[:, k, m * 128:(m + 1) * 128], gfp[:, k:k + 1])
                                for k in range(DT)],
                         out_ap=ps_w.ap[:, 0:1], deps=[mlv])
                lastb1.append(act(gh1[:, m:m + 1], ps_w.ap[:, 0:1], AF.Gelu,
                                  bias=gb1[:, m:m + 1], src_bank=ps_w))
            l2 = []
            for m in range(2):
                wop = mm(ps_w2, [(g2w[:, k, m * 128:(m + 1) * 128], gh1[:, k:k + 1])
                                 for k in range(4)],
                         out_ap=ps_w2.ap[:, 0:1], deps=lastb1)
                l2.append(act(gh2[:, m:m + 1], ps_w2.ap[:, 0:1], AF.Gelu,
                              bias=gb2[:, m:m + 1], src_bank=ps_w2))
            wop = mm(ps_w, [(g3w[:, k, 0:1], gh2[:, k:k + 1]) for k in range(2)],
                     out_ap=ps_w.ap[0:1, 0:1], deps=l2)
            gfs = act(gft[0:1, :], ps_w.ap[0:1, 0:1], AF.Sigmoid,
                      bias=gb3[0:1, 0:1], src_bank=ps_w)
            kk1 = vts(k1t[0:1, :], art[0:1, :], 0.6, OP.mult)
            t1_ = act(k0t[0:1, :], gft[0:1, :], AF.Copy, bias=-0.5, scale=0.7,
                      deps=[gfs])
            t2_ = vtt(k0t[0:1, :], k0t[0:1, :], art[0:1, :], OP.mult, deps=[t1_])
            t3_ = act(k0t[0:1, :], k0t[0:1, :], AF.Copy, bias=0.1, scale=2.0,
                      deps=[t2_])
            p.barrier()

            TW_SRC = [4, 5, 6, 7, 0, 1, 2, 3]
            TW_SGN = [+1, +1, -1, -1, -1, -1, +1, +1]
            for ck in range(nchf):
                t0 = ck * TCF
                lnb_w = ln_chunk(t0, TCF, tmp8, t1b, lnb)
                for cyc in range(3):
                    h1ops = []
                    for m in range(DT):
                        bk = ps_w if m % 2 == 0 else ps_w2
                        wop = mm(bk, [(w1[:, k, m * 128:(m + 1) * 128], lnb[:, k, :])
                                      for k in range(DT)], out_ap=bk.ap[:, 0:TCF],
                                 deps=lnb_w)
                        h1ops.append(act(h1[:, m, :], bk.ap[:, 0:TCF], AF.Gelu,
                                         bias=b1[:, m:m + 1], src_bank=bk))
                    h2ops = []
                    for m in range(4):
                        bk = ps_w if m % 2 == 0 else ps_w2
                        wop = mm(bk, [(w2[:, k, m * 128:(m + 1) * 128], h1[:, k, :])
                                      for k in range(DT)], out_ap=bk.ap[:, 0:TCF],
                                 deps=h1ops)
                        h2ops.append(act(h2[:, m, :], bk.ap[:, 0:TCF], AF.Gelu,
                                         bias=b2[:, m:m + 1], src_bank=bk))
                    h3ops = []
                    for m in range(2):
                        bk = ps_w if m % 2 == 0 else ps_w2
                        wop = mm(bk, [(w3[:, k, m * 128:(m + 1) * 128], h2[:, k, :])
                                      for k in range(4)], out_ap=bk.ap[:, 0:TCF],
                                 deps=h2ops)
                        h3ops.append(act(h3[:, m, :], bk.ap[:, 0:TCF], AF.Gelu,
                                         bias=b3[:, m:m + 1], src_bank=bk))
                    wop = mm(ps_w2, [(w4[:, k, :], h3[:, k, :]) for k in range(2)],
                             out_ap=ps_w2.ap[0:1, 0:TCF], deps=h3ops)
                    tf = smalls[0:1, 512:512 + TCF]
                    cpl = smalls[0:1, 1024:1024 + TCF]
                    a1 = act(tf, ps_w2.ap[0:1, 0:TCF], AF.Sigmoid,
                             bias=b4[0:1, 0:1], src_bank=ps_w2)
                    a2 = p.op("v", lambda e, o=cpl, i=tf: e.tensor_scalar(
                        o, i, k1t[0:1, 0:1], k0t[0:1, 0:1], OP.mult, OP.add),
                        deps=[a1, kk1, t3_])
                    rr = mm(ps_a, [(ones_row, cpl)],
                            out_ap=ps_a.ap[:, 0:TCF], deps=[a2])
                    tops = []
                    for dt in range(DT):
                        tops.append(vtt(tmp8[:, dt, :], lnb[:, TW_SRC[dt], :],
                                        ps_a.ap[:, 0:TCF], OP.mult,
                                        deps=lnb_w, banks=[ps_a]))
                    lnb_w = []
                    for dt in range(DT):
                        lnb_w.append(vtt(lnb[:, dt, :], lnb[:, dt, :], tmp8[:, dt, :],
                                         OP.add if TW_SGN[dt] > 0 else OP.subtract,
                                         deps=tops))
                for dt in range(DT):
                    vstt(resid[:, dt, t0:t0 + TCF], lnb[:, dt, :], 0.5,
                         resid[:, dt, t0:t0 + TCF], OP.mult, OP.add, deps=lnb_w)
                p.barrier()

        def attn_entry():
            A.reset(0)
            tmp = A.al(DT, TCA)
            sqt = A.al(DT, TCA)
            stash = [dma("g", xstash[dt * 128:(dt + 1) * 128, :], resid[:, dt, :])
                     for dt in range(DT)]
            for ck in range(TLOC // TCA):
                ln_chunk_into_resid(ck * TCA, TCA, sqt, tmp, stash)
                p.barrier()

        def ln_chunk_into_resid(t0, tc, sqbuf, t1buf, stash):
            sq_ops = [act(sqbuf[:, dt, 0:tc], resid[:, dt, t0:t0 + tc], AF.Square)
                      for dt in range(DT)]
            w_sx = mm(ps_w, [(ones_col, resid[:, dt, t0:t0 + tc]) for dt in range(DT)],
                      out_ap=ps_w.ap[0:1, 0:tc])
            w_sx2 = mm(ps_w2, [(ones_col, sqbuf[:, dt, 0:tc]) for dt in range(DT)],
                       out_ap=ps_w2.ap[0:1, 0:tc], deps=sq_ops)
            mu = smalls[0:1, 0:tc]
            ex2 = smalls[0:1, 512:512 + tc]
            var = smalls[0:1, 1024:1024 + tc]
            rstd = smalls[0:1, 1536:1536 + tc]
            e1 = vts(mu, ps_w.ap[0:1, 0:tc], 1.0 / D, OP.mult, banks=[ps_w])
            e2_ = vts(ex2, ps_w2.ap[0:1, 0:tc], 1.0 / D, OP.mult, banks=[ps_w2])
            e3 = vtt(var, mu, mu, OP.mult, deps=[e1])
            e4 = vtt(var, ex2, var, OP.subtract, deps=[e2_, e3])
            e5a = act(rstd, var, AF.Ln, bias=constf(EPS)[0:1, 0:1], deps=[e4])
            e5 = act(rstd, rstd, AF.Exp, scale=constf(-0.5)[0:1, 0:1], deps=[e5a])
            r1 = mm(ps_a, [(ones_row, mu)], out_ap=ps_a.ap[:, 0:tc], deps=[e3])
            r2 = mm(ps_b, [(ones_row, rstd)], out_ap=ps_b.ap[:, 0:tc], deps=[e5])
            outs = []
            for dt in range(DT):
                s1 = vtt(t1buf[:, dt, 0:tc], resid[:, dt, t0:t0 + tc], ps_a.ap[:, 0:tc],
                         OP.subtract, deps=[sq_ops[dt], w_sx] + stash, banks=[ps_a])
                outs.append(vstt(resid[:, dt, t0:t0 + tc], t1buf[:, dt, 0:tc], 1.0,
                                 ps_b.ap[:, 0:tc], OP.bypass, OP.mult,
                                 deps=[s1], banks=[ps_b]))
            return outs

        def mamba_layer(i):
            W = lwin[i]
            A.reset(0)
            selt = A.al(2 * NST, 128)
            inwT = A.al(DT, 2 * ESL)
            outwT = A.al(E2, D)
            xpwT = A.al(E2, 96)
            dtwT = A.al(ESL)
            wl = []
            for k in range(DT):
                wl.append(dma("y", inwT[:, k, :], W["inwT"][k * 128:(k + 1) * 128, :]))
            for k in range(E2):
                wl.append(dma("y", outwT[:, k, :], W["outwT"][k * 128:(k + 1) * 128, :]))
                wl.append(dma("y", xpwT[:, k, :], W["xpwT"][k * 128:(k + 1) * 128, :]))
            wl.append(dma("y", dtwT[0:RDT, :], W["dtwT"]))
            wl.append(dma("y", selt.rearrange("p a b -> p (a b)"), sel_in))
            wl.append(dma("y", convw_c.rearrange("p e k -> p (e k)"), W["convw"]))
            wl.append(dma("y", convb_c, W["convb"]))
            wl.append(dma("y", dtb_raw, W["dtb"]))
            wl.append(dma("y", pA_raw, W["alog"]))
            wl.append(dma("y", dp_c, W["dp"]))
            wl.append(dma("y", normw_c, W["normw"]))
            pa0 = act(pA.rearrange("p e n -> p (e n)"), pA_raw, AF.Exp, deps=wl)
            pa1 = vts(pA.rearrange("p e n -> p (e n)"), pA.rearrange("p e n -> p (e n)"),
                      -1.0, OP.mult, deps=[pa0])
            p.barrier()

            ub = A.al(E2, TCP)
            zgb = A.al(E2, TCP)
            xc = A.al(E2, TCP)
            dbcc = A.al(TCP)
            ckpt2 = A.off

            tmpa = A.al(DT, TCA)
            sqa = A.al(DT, TCA)
            for ck in range(TLOC // TCA):
                t0 = ck * TCA
                sq_ops = [act(sqa[:, dt, 0:TCA], resid[:, dt, t0:t0 + TCA], AF.Square)
                          for dt in range(DT)]
                w_sx2 = mm(ps_w, [(ones_col, sqa[:, dt, 0:TCA]) for dt in range(DT)],
                           out_ap=ps_w.ap[0:1, 0:TCA], deps=sq_ops)
                rstd = smalls[0:1, 0:TCA]
                e5a = act(rstd, ps_w.ap[0:1, 0:TCA], AF.Ln,
                          scale=constf(1.0 / D)[0:1, 0:1], bias=constf(EPS)[0:1, 0:1],
                          src_bank=ps_w)
                e5 = act(rstd, rstd, AF.Exp, scale=constf(-0.5)[0:1, 0:1], deps=[e5a])
                r2 = mm(ps_b, [(ones_row, rstd)], out_ap=ps_b.ap[:, 0:TCA],
                        deps=[e5])
                for dt in range(DT):
                    sop = vstt(tmpa[:, dt, 0:TCA], resid[:, dt, t0:t0 + TCA],
                               normw_c[:, dt:dt + 1], ps_b.ap[:, 0:TCA],
                               OP.mult, OP.mult, deps=[sq_ops[dt]], banks=[ps_b])
                    dma("g", xn_loc[dt * 128:(dt + 1) * 128, t0:t0 + TCA],
                        tmpa[:, dt, 0:TCA], deps=[sop])
                p.barrier()

            ccg = coll("AllGather", OP.bypass, [list(range(8))], xn_loc, xn_all,
                       deps=[p.last["g"]])
            p.barrier()

            A.reset(ckpt2)
            xn_in = A.al(DT, TCP)
            xi = A.al(E2, TCP + 3)
            for k in range(NCH):
                rk, toff = k // 2, (k % 2) * TCP
                xl = [dma("y", xn_in[:, dt, :],
                          xn_all[rk * D + dt * 128: rk * D + (dt + 1) * 128,
                                 toff:toff + TCP], deps=[ccg]) for dt in range(DT)]
                ev_xi = []
                ev_zg = []
                for m in range(4):
                    for tf in range(TCP // 512):
                        bk = ps_w if (m * 2 + tf) % 2 == 0 else ps_w2
                        wop = mm(bk, [(inwT[:, kk, m * 128:(m + 1) * 128],
                                       xn_in[:, kk, tf * 512:(tf + 1) * 512])
                                      for kk in range(DT)],
                                 out_ap=bk.ap[:, 0:512], deps=xl)
                        if m < 2:
                            ev_xi.append(act(xi[:, m, 3 + tf * 512: 3 + (tf + 1) * 512],
                                             bk.ap[:, 0:512], AF.Copy, src_bank=bk))
                        else:
                            ev_zg.append(act(zgb[:, m - 2, tf * 512:(tf + 1) * 512],
                                             bk.ap[:, 0:512], AF.Silu, src_bank=bk))
                halo = []
                if k % (NCH // 2) == 0:
                    halo = [vmemset(xi[:, :, 0:3], 0.0)]
                cops = []
                for e2 in range(E2):
                    c0 = vts(xc[:, e2, :], xi[:, e2, 0:TCP], convw_c[:, e2, 0:1],
                             OP.mult, deps=ev_xi + halo)
                    for kk in range(1, KCV):
                        c0 = vstt(xc[:, e2, :], xi[:, e2, kk:kk + TCP],
                                  convw_c[:, e2, kk:kk + 1], xc[:, e2, :],
                                  OP.mult, OP.add, deps=[c0])
                    cops.append(c0)
                uops = [act(ub[:, e2, :], xc[:, e2, :], AF.Silu,
                            bias=convb_c[:, e2:e2 + 1], deps=[cops[e2]])
                        for e2 in range(E2)]
                if k % (NCH // 2) != NCH // 2 - 1:
                    vcopy(xi[:, :, 0:3], xi[:, :, TCP:TCP + 3], deps=cops)
                dops = []
                for tf in range(TCP // 512):
                    wop = mm(ps_a, [(xpwT[:, e2, :], ub[:, e2, tf * 512:(tf + 1) * 512])
                                    for e2 in range(E2)],
                             out_ap=ps_a.ap[0:96, 0:512], deps=uops)
                    dops.append(act(dbcc[0:96, tf * 512:(tf + 1) * 512],
                                    ps_a.ap[0:96, 0:512], AF.Copy, src_bank=ps_a))
                dma("g", u_st.rearrange("(e pp) T -> pp e T", pp=128)[:, :, k * TCP:(k + 1) * TCP],
                    ub[:, :, :], deps=uops)
                dma("g", zg_st.rearrange("(e pp) T -> pp e T", pp=128)[:, :, k * TCP:(k + 1) * TCP],
                    zgb[:, :, :], deps=ev_zg)
                dma("g", dbc_i[:, k * TCP:(k + 1) * TCP], dbcc[0:96, 0:TCP], deps=dops)
                p.barrier()

            ccr = coll("AllReduce", OP.add, [list(range(8))], dbc_i, dbc_o,
                       deps=[p.last["g"]])
            p.barrier()

            A.reset(ckpt2)
            lb = A.al(E2, TCP)
            lub = A.al(E2, TCP)
            bct = A.al(TCP)
            _op0 = A.off
            dab = A.al(TCP)
            dxb = A.al(TCP)
            hb = A.al(TCP)
            g1b = A.al(TCP)
            ogb = A.al(E2, TCP)
            opst = arena[:, _op0:_op0 + DT * 512].rearrange("p (a b) -> p a b", a=DT)
            carry = smalls[:, 1900:1900 + E2 * NST].rearrange("p (e n) -> p e n", e=E2)
            for k in range(NCH):
                rk, toff = k // 2, (k % 2) * TCP
                l_u = dma("y", ub[:, :, :],
                          u_st.rearrange("(e pp) T -> pp e T", pp=128)[:, :, k * TCP:(k + 1) * TCP])
                l_z = dma("y", zgb[:, :, :],
                          zg_st.rearrange("(e pp) T -> pp e T", pp=128)[:, :, k * TCP:(k + 1) * TCP])
                l_d = dma("y", dbcc[0:96, 0:TCP], dbc_o[:, k * TCP:(k + 1) * TCP],
                          deps=[ccr])
                l_bc = dma("y", bct[0:32, 0:TCP], dbc_o[64:96, k * TCP:(k + 1) * TCP],
                           deps=[ccr])
                ld4 = [l_u, l_z, l_d, l_bc]
                lops = []
                for e2 in range(E2):
                    for tf in range(TCP // 512):
                        bk = ps_w if tf % 2 == 0 else ps_w2
                        wop = mm(bk, [(dtwT[0:RDT, e2 * 128:(e2 + 1) * 128],
                                       dbcc[0:RDT, tf * 512:(tf + 1) * 512])],
                                 out_ap=bk.ap[:, 0:512], deps=ld4)
                        lops.append(act(lb[:, e2, tf * 512:(tf + 1) * 512],
                                        bk.ap[:, 0:512], AF.Abs,
                                        bias=dtb_raw[:, e2:e2 + 1], src_bank=bk))
                        lops.append(act(xc[:, e2, tf * 512:(tf + 1) * 512],
                                        bk.ap[:, 0:512], AF.Relu,
                                        bias=dtb_raw[:, e2:e2 + 1], src_bank=bk))
                ee = act(lb.rearrange("p e t -> p (e t)"),
                         lb.rearrange("p e t -> p (e t)"), AF.Exp, scale=-1.0,
                         deps=lops)
                ll = ee
                tmpd = arena[:, _op0:_op0 + TCP]
                ltd = arena[:, _op0 + 2 * TCP:_op0 + 3 * TCP]
                for e2 in range(E2):
                    t1 = vts(tmpd, lb[:, e2, :], 1.0, OP.add, deps=[ee])
                    t2 = act(ltd, tmpd, AF.Ln, deps=[t1])
                    t3 = vstt(tmpd, tmpd, -1.0, lb[:, e2, :], OP.mult, OP.add,
                              deps=[t2])
                    t4 = vts(tmpd, tmpd, 1.0, OP.add, deps=[t3])
                    t5 = vtt(xc[:, e2, :], xc[:, e2, :], ltd, OP.add, deps=[t2, t4])
                    ll = vtt(lb[:, e2, :], xc[:, e2, :], tmpd, OP.add, deps=[t5])
                lu_ = vtt(lub.rearrange("p e t -> p (e t)"),
                          lb.rearrange("p e t -> p (e t)"),
                          ub.rearrange("p e t -> p (e t)"), OP.mult,
                          deps=[ll] + ld4)
                gops = []
                prev_scan = None
                last_ymm = [None, None]
                for e2 in range(E2):
                    last_sum = None
                    for n in range(NST):
                        da_op = act(dab[:, 0:TCP], lb[:, e2, :], AF.Exp,
                                    scale=pA[:, e2, n:n + 1],
                                    deps=[lu_, pa1, prev_scan])
                        dx_ops = []
                        for tf in range(TCP // 512):
                            br = mm(ps_a, [(selt[0:32, n, :],
                                            bct[0:32, tf * 512:(tf + 1) * 512])],
                                    out_ap=ps_a.ap[:, 0:512], deps=ld4)
                            dx_ops.append(vtt(dxb[:, tf * 512:(tf + 1) * 512],
                                              lub[:, e2, tf * 512:(tf + 1) * 512],
                                              ps_a.ap[:, 0:512], OP.mult,
                                              deps=[lu_, last_ymm[tf]], banks=[ps_a]))
                        init = zeros_col[:, 0:1] if k % (NCH // 2) == 0 \
                            else carry[:, e2, n:n + 1]
                        sc = vscan(hb[:, 0:TCP], dab[:, 0:TCP], dxb[:, 0:TCP], init,
                                   deps=[da_op] + dx_ops)
                        prev_scan = sc
                        cy = None
                        for tf in range(TCP // 512):
                            cr = mm(ps_b, [(selt[0:32, NST + n, :],
                                            bct[0:32, tf * 512:(tf + 1) * 512])],
                                    out_ap=ps_b.ap[:, 0:512], deps=ld4)
                            pr = vtt(dxb[:, tf * 512:(tf + 1) * 512],
                                     hb[:, tf * 512:(tf + 1) * 512],
                                     ps_b.ap[:, 0:512], OP.mult,
                                     deps=[sc], banks=[ps_b])
                            if tf == 0:
                                cy = vcopy(carry[:, e2, n:n + 1],
                                           hb[:, TCP - 1:TCP], deps=[pr])
                            last_sum = mm(ps_y, [(ident[:, 0:128],
                                                  dxb[:, tf * 512:(tf + 1) * 512])],
                                          out_ap=ps_y.ap[:, tf * 512:(tf + 1) * 512],
                                          deps=[pr], start=(n == 0),
                                          stop=(n == NST - 1))
                            last_ymm[tf] = last_sum
                    for tf in range(TCP // 512):
                        g1 = vstt(g1b[:, tf * 512:(tf + 1) * 512],
                                  ub[:, e2, tf * 512:(tf + 1) * 512],
                                  dp_c[:, e2:e2 + 1],
                                  ps_y.ap[:, tf * 512:(tf + 1) * 512],
                                  OP.mult, OP.add, deps=[last_sum] + ld4,
                                  banks=[ps_y])
                        gops.append(vtt(ogb[:, e2, tf * 512:(tf + 1) * 512],
                                        g1b[:, tf * 512:(tf + 1) * 512],
                                        zgb[:, e2, tf * 512:(tf + 1) * 512],
                                        OP.mult, deps=[g1] + ld4))
                for tf in range(TCP // 512):
                    evs = []
                    for m in range(DT):
                        bk = ps_w if m % 2 == 0 else ps_w2
                        wop = mm(bk, [(outwT[:, e2, m * 128:(m + 1) * 128],
                                       ogb[:, e2, tf * 512:(tf + 1) * 512])
                                      for e2 in range(E2)],
                                 out_ap=bk.ap[:, 0:512], deps=gops)
                        evs.append(act(opst[:, m, :], bk.ap[:, 0:512], AF.Copy,
                                       src_bank=bk))
                    for m in range(DT):
                        dma("g", op_i[rk * D + m * 128: rk * D + (m + 1) * 128,
                                      toff + tf * 512: toff + (tf + 1) * 512],
                            opst[:, m, :], deps=[evs[m]])
                p.barrier()

            ccs = coll("ReduceScatter", OP.add, [list(range(8))], op_i, op_o,
                       deps=[p.last["g"]])
            p.barrier()
            A.reset(ckpt2)
            radd = A.al(DT, TCP)
            for ck in range(TLOC // TCP):
                t0 = ck * TCP
                l1 = [dma("y", radd[:, dt, :], op_o[dt * 128:(dt + 1) * 128, t0:t0 + TCP],
                          deps=[ccs]) for dt in range(DT)]
                for dt in range(DT):
                    vtt(resid[:, dt, t0:t0 + TCP], resid[:, dt, t0:t0 + TCP],
                        radd[:, dt, :], OP.add, deps=l1)
                p.barrier()

        ffn_phase()
        p.barrier()
        attn_entry()
        p.barrier()
        mamba_layer(0)
        p.barrier()
        mamba_layer(1)
        p.barrier()
        A.reset(0)
        tmp3 = A.al(DT, TCP)
        for ck in range(TLOC // TCP):
            t0 = ck * TCP
            l1 = [dma("y", tmp3[:, dt, :], xstash[dt * 128:(dt + 1) * 128, t0:t0 + TCP])
                  for dt in range(DT)]
            for dt in range(DT):
                vtt(resid[:, dt, t0:t0 + TCP], resid[:, dt, t0:t0 + TCP],
                    tmp3[:, dt, :], OP.add, deps=l1)
            p.barrier()
        ffn_phase()
        p.barrier()
        for dt in range(DT):
            dma("g", out_d[dt * 128:(dt + 1) * 128, :], resid[:, dt, :])

        p.finalize()
        hw = {e: {e2: 0 for e2 in ENGS} for e in ENGS}

        def emit_for(ename, eng):
            for o in p.ops:
                if o.eng != ename:
                    continue
                for d in o.deps:
                    if d.eng == ename and not (d.dma or d.acoll):
                        continue
                    if d.val is None:
                        continue
                    if d.val > hw[ename][d.eng]:
                        eng.wait_ge(sems[d.eng], d.val)
                        hw[ename][d.eng] = d.val
                inst = o.emit(eng)
                if o.need_inc:
                    inst.then_inc(sems[o.eng], 16 if o.dma else 1)
            if ename == "g":
                eng.wait_ge(sems["g"], p.totals["g"])

        @block.sync
        def _(sync):
            emit_for("y", sync)

        @block.gpsimd
        def _(gpsimd):
            emit_for("g", gpsimd)

        @block.vector
        def _(vector):
            emit_for("v", vector)

        @block.scalar
        def _(scalar):
            emit_for("s", scalar)

        @block.tensor
        def _(tensor):
            emit_for("t", tensor)

    return nc


_NC_CACHE = [None]


def kernel(**inp):
    f32 = np.float32
    cc = np.ascontiguousarray
    if _NC_CACHE[0] is None:
        _NC_CACHE[0] = build_nc()
    nc = _NC_CACHE[0]
    ident = np.eye(128, dtype=f32)
    sel = np.zeros((128, 2 * NST * 128), dtype=f32)
    for n in range(NST):
        sel[n, n * 128:(n + 1) * 128] = 1.0
        sel[NST + n, (NST + n) * 128:(NST + n + 1) * 128] = 1.0
    x = np.asarray(inp["L0_attn_x"], dtype=f32)
    in_maps = []
    for c in range(8):
        b, q = c // 4, c % 4
        s0, s1 = ESL * c, ESL * (c + 1)
        m = {
            "x_dmaj": cc(x[b, q * TLOC:(q + 1) * TLOC, :].T),
            "ident_in": ident,
            "sel_in": sel,
            "cnw1": cc(np.asarray(inp["L0_ffn_cn_w1"], f32)),
            "cnb1": cc(np.asarray(inp["L0_ffn_cn_b1"], f32)[:, None]),
            "cnw2": cc(np.asarray(inp["L0_ffn_cn_w2"], f32)),
            "cnb2": cc(np.asarray(inp["L0_ffn_cn_b2"], f32)[:, None]),
            "cnw3": cc(np.asarray(inp["L0_ffn_cn_w3"], f32)),
            "cnb3": cc(np.asarray(inp["L0_ffn_cn_b3"], f32)[:, None]),
            "cnw4": cc(np.asarray(inp["L0_ffn_cn_w4"], f32)),
            "cnb4": cc(np.asarray(inp["L0_ffn_cn_b4"], f32)[:, None]),
            "gcw1": cc(np.asarray(inp["L0_ffn_gc_w1"], f32)),
            "gcb1": cc(np.asarray(inp["L0_ffn_gc_b1"], f32)[:, None]),
            "gcw2": cc(np.asarray(inp["L0_ffn_gc_w2"], f32)),
            "gcb2": cc(np.asarray(inp["L0_ffn_gc_b2"], f32)[:, None]),
            "gcw3": cc(np.asarray(inp["L0_ffn_gc_w3"], f32)),
            "gcb3": cc(np.asarray(inp["L0_ffn_gc_b3"], f32)[:, None]),
            "ar": cc(np.asarray(inp["L0_ffn_adaptive_range"], f32)[:, None]),
        }
        for i in range(2):
            inw = np.asarray(inp["L0_attn_in_proj_w"], f32)[i]
            rows = np.concatenate([inw[s0:s1, :], inw[ED + s0:ED + s1, :]], axis=0)
            m[f"l{i}_inwT"] = cc(rows.T)
            m[f"l{i}_outwT"] = cc(np.asarray(inp["L0_attn_out_proj_w"], f32)[i][:, s0:s1].T)
            m[f"l{i}_xpwT"] = cc(np.asarray(inp["L0_attn_x_proj_w"], f32)[i][:, s0:s1].T)
            m[f"l{i}_dtwT"] = cc(np.asarray(inp["L0_attn_dt_w"], f32)[i][s0:s1, :].T)
            cw = np.asarray(inp["L0_attn_conv_w"], f32)[i][s0:s1, 0, :]
            m[f"l{i}_convw"] = cc(cw.reshape(E2, 128, KCV).transpose(1, 0, 2).reshape(128, E2 * KCV))
            m[f"l{i}_convb"] = cc(np.asarray(inp["L0_attn_conv_b"], f32)[i][s0:s1].reshape(E2, 128).T)
            m[f"l{i}_dtb"] = cc(np.asarray(inp["L0_attn_dt_b"], f32)[i][s0:s1].reshape(E2, 128).T)
            al = np.asarray(inp["L0_attn_A_log"], f32)[i][s0:s1, :]
            m[f"l{i}_alog"] = cc(al.reshape(E2, 128, NST).transpose(1, 0, 2).reshape(128, E2 * NST))
            m[f"l{i}_dp"] = cc(np.asarray(inp["L0_attn_D_p"], f32)[i][s0:s1].reshape(E2, 128).T)
            m[f"l{i}_normw"] = cc(np.asarray(inp["L0_attn_norm_w"], f32)[i].reshape(DT, 128).T)
        in_maps.append(m)
    res = run_bass_kernel_spmd(nc, in_maps, core_ids=list(range(8)))
    out = np.empty((B, L, D), dtype=f32)
    for c in range(8):
        b, q = c // 4, c % 4
        out[b, q * TLOC:(q + 1) * TLOC, :] = np.asarray(res.results[c]["out_dmaj"]).T
    return out


# version 11
# speedup vs baseline: 1.5377x; 1.5377x over previous
import numpy as np
import concourse.bass as bass
import concourse.mybir as mybir
from concourse.bass_utils import run_bass_kernel_spmd
from concourse.alu_op_type import AluOpType as OP

F32 = mybir.dt.float32
AF = mybir.ActivationFunctionType

B, L, D, ED, NST, KCV, RDT = 2, 8192, 1024, 2048, 16, 4, 64
TLOC = L // 4
ESL = ED // 8
BT = B * L
DT = D // 128
E2 = ESL // 128
TCA = 512
TCP = 1024
NCH = BT // TCP
TCF = 256
EPS = 1e-5

ENGS = ("y", "g", "v", "s", "t")


class Op:
    __slots__ = ("eng", "emit", "deps", "dma", "need_inc", "val", "acoll")

    def __init__(self, eng, emit, deps, dma, acoll=False):
        self.eng, self.emit, self.deps, self.dma = eng, emit, list(deps), dma
        self.acoll = acoll
        self.need_inc = dma or acoll
        self.val = None


class Plan:
    def __init__(self):
        self.ops = []
        self.last = {e: None for e in ENGS}
        self.bdeps = {e: [] for e in ENGS}

    def op(self, eng, emit, deps=(), dma=False, acoll=False):
        o = Op(eng, emit, [d for d in deps if d is not None], dma, acoll)
        if self.bdeps[eng]:
            o.deps.extend(self.bdeps[eng])
            self.bdeps[eng] = []
        self.ops.append(o)
        self.last[eng] = o
        return o

    def barrier(self):
        snap = [o for o in self.last.values() if o is not None]
        for e in ENGS:
            self.bdeps[e] = list(snap)

    def finalize(self):
        for o in self.ops:
            for d in o.deps:
                if d.eng != o.eng or d.dma or d.acoll:
                    d.need_inc = True
        ctr = {e: 0 for e in ENGS}
        for o in self.ops:
            if o.need_inc:
                ctr[o.eng] += 16 if o.dma else 1
                o.val = ctr[o.eng]
        self.totals = ctr


class PB:

    def __init__(self, ap):
        self.ap = ap
        self.readers = []
        self.writer = None


def build_nc():
    nc = bass.Bass("TRN2", target_bir_lowering=False)

    x_in = nc.dram_tensor("x_dmaj", [D, TLOC], F32, kind="ExternalInput").ap()
    id_in = nc.dram_tensor("ident_in", [128, 128], F32, kind="ExternalInput").ap()
    sel_in = nc.dram_tensor("sel_in", [128, 2 * NST * 128], F32, kind="ExternalInput").ap()
    out_d = nc.dram_tensor("out_dmaj", [D, TLOC], F32, kind="ExternalOutput").ap()
    lwin = []
    for i in range(2):
        lwin.append({k: nc.dram_tensor(f"l{i}_{k}", shp, F32, kind="ExternalInput").ap()
                     for k, shp in [("inwT", [D, 2 * ESL]), ("outwT", [ESL, D]),
                                    ("xpwT", [ESL, 96]), ("dtwT", [RDT, ESL]),
                                    ("convw", [128, E2 * KCV]), ("convb", [128, E2]),
                                    ("dtb", [128, E2]), ("alog", [128, E2 * NST]),
                                    ("dp", [128, E2]), ("normw", [128, DT])]})
    fwin = {nm: nc.dram_tensor(nm, shp, F32, kind="ExternalInput").ap()
            for nm, shp in [("cnw1", [D, D]), ("cnb1", [D, 1]),
                            ("cnw2", [D, D // 2]), ("cnb2", [D // 2, 1]),
                            ("cnw3", [D // 2, D // 4]), ("cnb3", [D // 4, 1]),
                            ("cnw4", [D // 4, 1]), ("cnb4", [1, 1]),
                            ("gcw1", [D, D // 2]), ("gcb1", [D // 2, 1]),
                            ("gcw2", [D // 2, D // 4]), ("gcb2", [D // 4, 1]),
                            ("gcw3", [D // 4, 1]), ("gcb3", [1, 1]), ("ar", [1, 1])]}
    xn_loc = nc.dram_tensor("xn_loc", [D, TLOC], F32, kind="Internal").ap()
    xn_all = nc.dram_tensor("xn_all", [8 * D, TLOC], F32, kind="Internal").ap()
    u_st = nc.dram_tensor("u_st", [ESL, BT], F32, kind="Internal").ap()
    zg_st = nc.dram_tensor("zg_st", [ESL, BT], F32, kind="Internal").ap()
    dbc_i = nc.dram_tensor("dbc_i", [96, BT], F32, kind="Internal").ap()
    dbc_o = nc.dram_tensor("dbc_o", [96, BT], F32, kind="Internal").ap()
    op_i = nc.dram_tensor("op_i", [8 * D, TLOC], F32, kind="Internal").ap()
    op_o = nc.dram_tensor("op_o", [D, TLOC], F32, kind="Internal").ap()
    gf_i = nc.dram_tensor("gf_i", [128, DT], F32, kind="Internal").ap()
    gf_o = nc.dram_tensor("gf_o", [128, DT], F32, kind="Internal").ap()
    xstash = nc.dram_tensor("xstash", [D, TLOC], F32, kind="Internal").ap()

    from contextlib import ExitStack
    with ExitStack() as stk:
        ec = stk.enter_context
        resid = ec(nc.sbuf_tensor("resid", [128, DT, TLOC], F32))
        smalls = ec(nc.sbuf_tensor("smalls", [128, 2048], F32))
        consts = ec(nc.sbuf_tensor("consts", [128, 512], F32))
        ident = ec(nc.sbuf_tensor("ident", [128, 128], F32))
        ones_t = ec(nc.sbuf_tensor("ones_t", [128, 128], F32))
        arena = ec(nc.sbuf_tensor("arena", [128, 29696], F32))
        ps_w = PB(ec(nc.psum_tensor("ps_w", [128, 512], F32)))
        ps_w2 = PB(ec(nc.psum_tensor("ps_w2", [128, 512], F32)))
        ps_a = PB(ec(nc.psum_tensor("ps_a", [128, 512], F32)))
        ps_b = PB(ec(nc.psum_tensor("ps_b", [128, 512], F32)))
        ps_c = PB(ec(nc.psum_tensor("ps_c", [128, 512], F32)))
        ps_d = PB(ec(nc.psum_tensor("ps_d", [128, 512], F32)))
        ps_e = PB(ec(nc.psum_tensor("ps_e", [128, 512], F32)))
        ps_f = PB(ec(nc.psum_tensor("ps_f", [128, 512], F32)))
        sems = {e: ec(nc.semaphore(f"sem_{e}")) for e in ENGS}
        block = ec(nc.Block())

        p = Plan()

        c_off = [0]

        def calloc(n):
            o = c_off[0]
            c_off[0] += n
            assert c_off[0] <= 256
            return consts[:, o:o + n]

        pA = calloc(E2 * NST).rearrange("p (e n) -> p e n", e=E2)
        pA_raw = calloc(E2 * NST)
        dtb_raw = calloc(E2)
        convw_c = calloc(E2 * KCV).rearrange("p (e k) -> p e k", e=E2)
        convb_c = calloc(E2)
        dp_c = calloc(E2)
        normw_c = calloc(DT)
        ones_col = calloc(1)
        zeros_col = calloc(1)
        ones_row = ones_t[0:1, :]
        k0t = calloc(1)
        k1t = calloc(1)
        art = calloc(1)
        gft = calloc(1)
        carry_c = calloc(E2 * NST)
        _cmap = {}

        def constf(v):
            if v not in _cmap:
                _cmap[v] = calloc(1)
                _cinit.append(vmemset(_cmap[v], float(v)))
            return _cmap[v]
        _cinit = []

        class Ar:
            def __init__(self):
                self.off = 0

            def al(self, *shape):
                n = int(np.prod(shape))
                o = self.off
                self.off += n
                assert self.off <= 29696, f"arena overflow {self.off}"
                v = arena[:, o:o + n]
                if len(shape) == 2:
                    v = v.rearrange("p (a b) -> p a b", a=shape[0])
                return v

            def reset(self, off=0):
                self.off = off

        A = Ar()

        def dma(q, out, in_, deps=()):
            return p.op(q, lambda e, o=out, i=in_: e.dma_start(out=o, in_=i),
                        deps=deps, dma=True)

        def mm(bank, pairs, out_ap=None, deps=(), start=True, stop=True):
            tgt = out_ap if out_ap is not None else bank.ap[:]
            ops = []
            nmm = len(pairs)
            for j, (lh, rh) in enumerate(pairs):
                d = list(deps) + (bank.readers if (j == 0 and start) else []) \
                    + ([bank.writer] if (j == 0 and not start and bank.writer) else [])
                st = start and j == 0
                sp = stop and j == nmm - 1
                ops.append(p.op("t", lambda e, lh=lh, rh=rh, st=st, sp=sp, t=tgt:
                                e.matmul(t, lh, rh, start=st, stop=sp), deps=d))
            if start:
                bank.readers = []
            bank.writer = ops[-1]
            return ops[-1]

        def rd(bank, op):
            bank.readers.append(op)
            return op

        def act(out, in_, func, deps=(), src_bank=None, **kw):
            d = list(deps) + ([src_bank.writer] if src_bank else [])
            o = p.op("s", lambda e, o=out, i=in_, f=func, k=dict(kw):
                     e.activation(out=o, in_=i, func=f, **k), deps=d)
            if src_bank:
                rd(src_bank, o)
            return o

        def vtt(out, a, b, op_, deps=(), banks=()):
            d = list(deps) + [bk.writer for bk in banks if bk.writer]
            o = p.op("v", lambda e, o=out, x=a, y2=b, z=op_:
                     e.tensor_tensor(o, x, y2, z), deps=d)
            for bk in banks:
                rd(bk, o)
            return o

        def vts(out, a, s1, op_, deps=(), banks=()):
            d = list(deps) + [bk.writer for bk in banks if bk.writer]
            o = p.op("v", lambda e, o=out, x=a, s=s1, z=op_:
                     e.tensor_scalar(o, x, s, None, z), deps=d)
            for bk in banks:
                rd(bk, o)
            return o

        def vstt(out, in0, scal, in1, op0, op1, deps=(), banks=(), accum_out=None):
            d = list(deps) + [bk.writer for bk in banks if bk.writer]

            def f(e, o=out, i0=in0, sc=scal, i1=in1, a=op0, b2=op1, ac=accum_out):
                if ac is not None:
                    return e.scalar_tensor_tensor(o, i0, sc, i1, a, b2, accum_out=ac)
                return e.scalar_tensor_tensor(o, i0, sc, i1, a, b2)
            o = p.op("v", f, deps=d)
            for bk in banks:
                rd(bk, o)
            return o

        def vscan(out, d0, d1, init, deps=()):
            return p.op("v", lambda e, o=out, a=d0, b2=d1, i=init:
                        e.tensor_tensor_scan(o, a, b2, i, OP.mult, OP.add), deps=deps)

        def vcopy(out, in_, deps=()):
            return p.op("v", lambda e, o=out, i=in_: e.tensor_copy(o, i), deps=deps)

        def vred(out, in_, deps=()):
            return p.op("v", lambda e, o=out, i=in_:
                        e.tensor_reduce(out=o, in_=i, axis=mybir.AxisListType.X,
                                        op=OP.add), deps=deps)

        def vmemset(ap, val, deps=()):
            return p.op("v", lambda e, a=ap, v=val: e.memset(a, v), deps=deps)

        def coll(kind, op_, groups, in_ap, out_ap, deps=()):
            return p.op("g", lambda e, k=kind, o=op_, g=groups, i=in_ap, u2=out_ap:
                        e.collective_compute(k, o, replica_groups=g,
                                             ins=[i.opt()], outs=[u2.opt()]),
                        deps=deps, acoll=True)

        for dt in range(DT):
            dma("y", resid[:, dt, :], x_in[dt * 128:(dt + 1) * 128, :])
        dma("y", ident[:, :], id_in)
        vmemset(ones_col, 1.0)
        vmemset(ones_t[:, :], 1.0)
        vmemset(zeros_col, 0.0)
        dma("y", art[0:1, :], fwin["ar"])
        for _v in (EPS, -0.5, 1.0 / D):
            constf(_v)
        p.barrier()

        def ln_chunk(t0, tc, sqbuf, t1buf, out_buf, gf_ck=None, gfparts=None,
                     extra_dep_per_dt=None):
            sq_ops = [act(sqbuf[:, dt, 0:tc], resid[:, dt, t0:t0 + tc], AF.Square)
                      for dt in range(DT)]
            w_sx = mm(ps_w, [(ones_col, resid[:, dt, t0:t0 + tc]) for dt in range(DT)],
                      out_ap=ps_w.ap[0:1, 0:tc])
            w_sx2 = mm(ps_w2, [(ones_col, sqbuf[:, dt, 0:tc]) for dt in range(DT)],
                       out_ap=ps_w2.ap[0:1, 0:tc], deps=sq_ops)
            mu = smalls[0:1, 0:tc]
            ex2 = smalls[0:1, 512:512 + tc]
            var = smalls[0:1, 1024:1024 + tc]
            rstd = smalls[0:1, 1536:1536 + tc]
            e1 = vts(mu, ps_w.ap[0:1, 0:tc], 1.0 / D, OP.mult, banks=[ps_w])
            e2_ = vts(ex2, ps_w2.ap[0:1, 0:tc], 1.0 / D, OP.mult, banks=[ps_w2])
            e3 = vtt(var, mu, mu, OP.mult, deps=[e1])
            e4 = vtt(var, ex2, var, OP.subtract, deps=[e2_, e3])
            e5a = act(rstd, var, AF.Ln, bias=constf(EPS)[0:1, 0:1], deps=[e4])
            e5 = act(rstd, rstd, AF.Exp, scale=constf(-0.5)[0:1, 0:1], deps=[e5a])
            r1 = mm(ps_a, [(ones_row, mu)], out_ap=ps_a.ap[:, 0:tc],
                    deps=[e3])
            r2 = mm(ps_b, [(ones_row, rstd)], out_ap=ps_b.ap[:, 0:tc],
                    deps=[e5])
            outs = []
            for dt in range(DT):
                xd = [extra_dep_per_dt[dt]] if extra_dep_per_dt else []
                s1 = vtt(t1buf[:, dt, 0:tc], resid[:, dt, t0:t0 + tc], ps_a.ap[:, 0:tc],
                         OP.subtract, deps=[sq_ops[dt]] + xd, banks=[ps_a])
                ac = gfparts[:, dt, gf_ck:gf_ck + 1] if gf_ck is not None else None
                outs.append(vstt(out_buf[:, dt, 0:tc], t1buf[:, dt, 0:tc], 1.0,
                                 ps_b.ap[:, 0:tc], OP.bypass, OP.mult,
                                 deps=[s1], banks=[ps_b], accum_out=ac))
            return outs

        def ffn_phase():
            A.reset(0)
            w1 = A.al(DT, D)
            w2 = A.al(DT, D // 2)
            w3 = A.al(4, D // 4)
            w4 = A.al(2, 1)
            g1w = A.al(DT, D // 2)
            g2w = A.al(4, D // 4)
            g3w = A.al(2, 1)
            b1 = A.al(DT)
            b2 = A.al(4)
            b3 = A.al(2)
            b4 = A.al(1)
            gb1 = A.al(4)
            gb2 = A.al(2)
            gb3 = A.al(1)
            lnb = A.al(DT, TCF)
            h1 = A.al(DT, TCF)
            h2 = A.al(4, TCF)
            h3 = A.al(2, TCF)
            tmp8 = A.al(DT, TCF)
            t1b = A.al(DT, TCF)
            gfparts = A.al(DT, DT)
            gfp = A.al(DT)

            for src, dst in [(fwin["cnw1"], w1), (fwin["cnw2"], w2), (fwin["cnw3"], w3),
                             (fwin["cnw4"], w4), (fwin["gcw1"], g1w), (fwin["gcw2"], g2w),
                             (fwin["gcw3"], g3w)]:
                for k in range(src.shape[0] // 128):
                    dma("y", dst[:, k, :], src[k * 128:(k + 1) * 128, :])
            for src, dst in [(fwin["cnb1"], b1), (fwin["cnb2"], b2), (fwin["cnb3"], b3),
                             (fwin["gcb1"], gb1), (fwin["gcb2"], gb2)]:
                for k in range(src.shape[0] // 128):
                    dma("y", dst[:, k:k + 1], src[k * 128:(k + 1) * 128, :])
            dma("y", b4[0:1, 0:1], fwin["cnb4"])
            dma("y", gb3[0:1, 0:1], fwin["gcb3"])
            p.barrier()

            nchf = TLOC // TCF
            for ck in range(nchf):
                ln_chunk(ck * TCF, TCF, tmp8, t1b, lnb, gf_ck=ck, gfparts=gfparts)
                p.barrier()
            rg = vred(gfp[:, 0:DT], gfparts)
            d1 = dma("g", gf_i, gfp[:, 0:DT], deps=[rg])
            cc1 = coll("AllReduce", OP.add, [[0, 1, 2, 3], [4, 5, 6, 7]],
                       gf_i, gf_o, deps=[d1])
            d2 = dma("y", gfp[:, 0:DT], gf_o, deps=[cc1])
            mlv = vts(gfp[:, 0:DT], gfp[:, 0:DT], 1.0 / L, OP.mult, deps=[d2])

            gh1 = smalls[0:128, 0:4]
            gh2 = smalls[0:128, 4:6]
            lastb1 = []
            for m in range(4):
                wop = mm(ps_w, [(g1w[:, k, m * 128:(m + 1) * 128], gfp[:, k:k + 1])
                                for k in range(DT)],
                         out_ap=ps_w.ap[:, 0:1], deps=[mlv])
                lastb1.append(act(gh1[:, m:m + 1], ps_w.ap[:, 0:1], AF.Gelu,
                                  bias=gb1[:, m:m + 1], src_bank=ps_w))
            l2 = []
            for m in range(2):
                wop = mm(ps_w2, [(g2w[:, k, m * 128:(m + 1) * 128], gh1[:, k:k + 1])
                                 for k in range(4)],
                         out_ap=ps_w2.ap[:, 0:1], deps=lastb1)
                l2.append(act(gh2[:, m:m + 1], ps_w2.ap[:, 0:1], AF.Gelu,
                              bias=gb2[:, m:m + 1], src_bank=ps_w2))
            wop = mm(ps_w, [(g3w[:, k, 0:1], gh2[:, k:k + 1]) for k in range(2)],
                     out_ap=ps_w.ap[0:1, 0:1], deps=l2)
            gfs = act(gft[0:1, :], ps_w.ap[0:1, 0:1], AF.Sigmoid,
                      bias=gb3[0:1, 0:1], src_bank=ps_w)
            kk1 = vts(k1t[0:1, :], art[0:1, :], 0.6, OP.mult)
            t1_ = act(k0t[0:1, :], gft[0:1, :], AF.Copy, bias=-0.5, scale=0.7,
                      deps=[gfs])
            t2_ = vtt(k0t[0:1, :], k0t[0:1, :], art[0:1, :], OP.mult, deps=[t1_])
            t3_ = act(k0t[0:1, :], k0t[0:1, :], AF.Copy, bias=0.1, scale=2.0,
                      deps=[t2_])
            p.barrier()

            TW_SRC = [4, 5, 6, 7, 0, 1, 2, 3]
            TW_SGN = [+1, +1, -1, -1, -1, -1, +1, +1]
            for ck in range(nchf):
                t0 = ck * TCF
                lnb_w = ln_chunk(t0, TCF, tmp8, t1b, lnb)
                for cyc in range(3):
                    h1ops = []
                    for m in range(DT):
                        bk = ps_w if m % 2 == 0 else ps_w2
                        wop = mm(bk, [(w1[:, k, m * 128:(m + 1) * 128], lnb[:, k, :])
                                      for k in range(DT)], out_ap=bk.ap[:, 0:TCF],
                                 deps=lnb_w)
                        h1ops.append(act(h1[:, m, :], bk.ap[:, 0:TCF], AF.Gelu,
                                         bias=b1[:, m:m + 1], src_bank=bk))
                    h2ops = []
                    for m in range(4):
                        bk = ps_w if m % 2 == 0 else ps_w2
                        wop = mm(bk, [(w2[:, k, m * 128:(m + 1) * 128], h1[:, k, :])
                                      for k in range(DT)], out_ap=bk.ap[:, 0:TCF],
                                 deps=h1ops)
                        h2ops.append(act(h2[:, m, :], bk.ap[:, 0:TCF], AF.Gelu,
                                         bias=b2[:, m:m + 1], src_bank=bk))
                    h3ops = []
                    for m in range(2):
                        bk = ps_w if m % 2 == 0 else ps_w2
                        wop = mm(bk, [(w3[:, k, m * 128:(m + 1) * 128], h2[:, k, :])
                                      for k in range(4)], out_ap=bk.ap[:, 0:TCF],
                                 deps=h2ops)
                        h3ops.append(act(h3[:, m, :], bk.ap[:, 0:TCF], AF.Gelu,
                                         bias=b3[:, m:m + 1], src_bank=bk))
                    wop = mm(ps_w2, [(w4[:, k, :], h3[:, k, :]) for k in range(2)],
                             out_ap=ps_w2.ap[0:1, 0:TCF], deps=h3ops)
                    tf = smalls[0:1, 512:512 + TCF]
                    cpl = smalls[0:1, 1024:1024 + TCF]
                    a1 = act(tf, ps_w2.ap[0:1, 0:TCF], AF.Sigmoid,
                             bias=b4[0:1, 0:1], src_bank=ps_w2)
                    a2 = p.op("v", lambda e, o=cpl, i=tf: e.tensor_scalar(
                        o, i, k1t[0:1, 0:1], k0t[0:1, 0:1], OP.mult, OP.add),
                        deps=[a1, kk1, t3_])
                    rr = mm(ps_a, [(ones_row, cpl)],
                            out_ap=ps_a.ap[:, 0:TCF], deps=[a2])
                    tops = []
                    for dt in range(DT):
                        tops.append(vtt(tmp8[:, dt, :], lnb[:, TW_SRC[dt], :],
                                        ps_a.ap[:, 0:TCF], OP.mult,
                                        deps=lnb_w, banks=[ps_a]))
                    lnb_w = []
                    for dt in range(DT):
                        lnb_w.append(vtt(lnb[:, dt, :], lnb[:, dt, :], tmp8[:, dt, :],
                                         OP.add if TW_SGN[dt] > 0 else OP.subtract,
                                         deps=tops))
                for dt in range(DT):
                    vstt(resid[:, dt, t0:t0 + TCF], lnb[:, dt, :], 0.5,
                         resid[:, dt, t0:t0 + TCF], OP.mult, OP.add, deps=lnb_w)
                p.barrier()

        def attn_entry():
            A.reset(0)
            tmp = A.al(DT, TCA)
            sqt = A.al(DT, TCA)
            stash = [dma("g", xstash[dt * 128:(dt + 1) * 128, :], resid[:, dt, :])
                     for dt in range(DT)]
            for ck in range(TLOC // TCA):
                ln_chunk_into_resid(ck * TCA, TCA, sqt, tmp, stash)
                p.barrier()

        def ln_chunk_into_resid(t0, tc, sqbuf, t1buf, stash):
            sq_ops = [act(sqbuf[:, dt, 0:tc], resid[:, dt, t0:t0 + tc], AF.Square)
                      for dt in range(DT)]
            w_sx = mm(ps_w, [(ones_col, resid[:, dt, t0:t0 + tc]) for dt in range(DT)],
                      out_ap=ps_w.ap[0:1, 0:tc])
            w_sx2 = mm(ps_w2, [(ones_col, sqbuf[:, dt, 0:tc]) for dt in range(DT)],
                       out_ap=ps_w2.ap[0:1, 0:tc], deps=sq_ops)
            mu = smalls[0:1, 0:tc]
            ex2 = smalls[0:1, 512:512 + tc]
            var = smalls[0:1, 1024:1024 + tc]
            rstd = smalls[0:1, 1536:1536 + tc]
            e1 = vts(mu, ps_w.ap[0:1, 0:tc], 1.0 / D, OP.mult, banks=[ps_w])
            e2_ = vts(ex2, ps_w2.ap[0:1, 0:tc], 1.0 / D, OP.mult, banks=[ps_w2])
            e3 = vtt(var, mu, mu, OP.mult, deps=[e1])
            e4 = vtt(var, ex2, var, OP.subtract, deps=[e2_, e3])
            e5a = act(rstd, var, AF.Ln, bias=constf(EPS)[0:1, 0:1], deps=[e4])
            e5 = act(rstd, rstd, AF.Exp, scale=constf(-0.5)[0:1, 0:1], deps=[e5a])
            r1 = mm(ps_a, [(ones_row, mu)], out_ap=ps_a.ap[:, 0:tc], deps=[e3])
            r2 = mm(ps_b, [(ones_row, rstd)], out_ap=ps_b.ap[:, 0:tc], deps=[e5])
            outs = []
            for dt in range(DT):
                s1 = vtt(t1buf[:, dt, 0:tc], resid[:, dt, t0:t0 + tc], ps_a.ap[:, 0:tc],
                         OP.subtract, deps=[sq_ops[dt], w_sx] + stash, banks=[ps_a])
                outs.append(vstt(resid[:, dt, t0:t0 + tc], t1buf[:, dt, 0:tc], 1.0,
                                 ps_b.ap[:, 0:tc], OP.bypass, OP.mult,
                                 deps=[s1], banks=[ps_b]))
            return outs

        def mamba_layer(i):
            W = lwin[i]
            A.reset(0)
            selt = A.al(2 * NST, 128)
            inwT = A.al(DT, 2 * ESL)
            outwT = A.al(E2, D)
            xpwT = A.al(E2, 96)
            dtwT = A.al(ESL)
            wl = []
            for k in range(DT):
                wl.append(dma("y", inwT[:, k, :], W["inwT"][k * 128:(k + 1) * 128, :]))
            for k in range(E2):
                wl.append(dma("y", outwT[:, k, :], W["outwT"][k * 128:(k + 1) * 128, :]))
                wl.append(dma("y", xpwT[:, k, :], W["xpwT"][k * 128:(k + 1) * 128, :]))
            wl.append(dma("y", dtwT[0:RDT, :], W["dtwT"]))
            wl.append(dma("y", selt.rearrange("p a b -> p (a b)"), sel_in))
            wl.append(dma("y", convw_c.rearrange("p e k -> p (e k)"), W["convw"]))
            wl.append(dma("y", convb_c, W["convb"]))
            wl.append(dma("y", dtb_raw, W["dtb"]))
            wl.append(dma("y", pA_raw, W["alog"]))
            wl.append(dma("y", dp_c, W["dp"]))
            wl.append(dma("y", normw_c, W["normw"]))
            pa0 = act(pA.rearrange("p e n -> p (e n)"), pA_raw, AF.Exp, deps=wl)
            pa1 = vts(pA.rearrange("p e n -> p (e n)"), pA.rearrange("p e n -> p (e n)"),
                      -1.0, OP.mult, deps=[pa0])
            p.barrier()

            ub = A.al(E2, TCP)
            zgb = A.al(E2, TCP)
            xc = A.al(E2, TCP)
            dbcc = A.al(TCP)
            ckpt2 = A.off

            tmpa = A.al(DT, TCA)
            sqa = A.al(DT, TCA)
            for ck in range(TLOC // TCA):
                t0 = ck * TCA
                sq_ops = [act(sqa[:, dt, 0:TCA], resid[:, dt, t0:t0 + TCA], AF.Square)
                          for dt in range(DT)]
                w_sx2 = mm(ps_w, [(ones_col, sqa[:, dt, 0:TCA]) for dt in range(DT)],
                           out_ap=ps_w.ap[0:1, 0:TCA], deps=sq_ops)
                rstd = smalls[0:1, 0:TCA]
                e5a = act(rstd, ps_w.ap[0:1, 0:TCA], AF.Ln,
                          scale=constf(1.0 / D)[0:1, 0:1], bias=constf(EPS)[0:1, 0:1],
                          src_bank=ps_w)
                e5 = act(rstd, rstd, AF.Exp, scale=constf(-0.5)[0:1, 0:1], deps=[e5a])
                r2 = mm(ps_b, [(ones_row, rstd)], out_ap=ps_b.ap[:, 0:TCA],
                        deps=[e5])
                for dt in range(DT):
                    sop = vstt(tmpa[:, dt, 0:TCA], resid[:, dt, t0:t0 + TCA],
                               normw_c[:, dt:dt + 1], ps_b.ap[:, 0:TCA],
                               OP.mult, OP.mult, deps=[sq_ops[dt]], banks=[ps_b])
                    dma("g", xn_loc[dt * 128:(dt + 1) * 128, t0:t0 + TCA],
                        tmpa[:, dt, 0:TCA], deps=[sop])
                p.barrier()

            ccg = coll("AllGather", OP.bypass, [list(range(8))], xn_loc, xn_all,
                       deps=[p.last["g"]])
            p.barrier()

            A.reset(ckpt2)
            xn_in = A.al(DT, TCP)
            xi = A.al(E2, TCP + 3)
            for k in range(NCH):
                rk, toff = k // 2, (k % 2) * TCP
                xl = [dma("y", xn_in[:, dt, :],
                          xn_all[rk * D + dt * 128: rk * D + (dt + 1) * 128,
                                 toff:toff + TCP], deps=[ccg]) for dt in range(DT)]
                ev_xi = []
                ev_zg = []
                for m in range(4):
                    for tf in range(TCP // 512):
                        bk = ps_w if (m * 2 + tf) % 2 == 0 else ps_w2
                        wop = mm(bk, [(inwT[:, kk, m * 128:(m + 1) * 128],
                                       xn_in[:, kk, tf * 512:(tf + 1) * 512])
                                      for kk in range(DT)],
                                 out_ap=bk.ap[:, 0:512], deps=xl)
                        if m < 2:
                            ev_xi.append(act(xi[:, m, 3 + tf * 512: 3 + (tf + 1) * 512],
                                             bk.ap[:, 0:512], AF.Copy, src_bank=bk))
                        else:
                            ev_zg.append(act(zgb[:, m - 2, tf * 512:(tf + 1) * 512],
                                             bk.ap[:, 0:512], AF.Silu, src_bank=bk))
                halo = []
                if k % (NCH // 2) == 0:
                    halo = [vmemset(xi[:, :, 0:3], 0.0)]
                cops = []
                for e2 in range(E2):
                    c0 = vts(xc[:, e2, :], xi[:, e2, 0:TCP], convw_c[:, e2, 0:1],
                             OP.mult, deps=ev_xi + halo)
                    for kk in range(1, KCV):
                        c0 = vstt(xc[:, e2, :], xi[:, e2, kk:kk + TCP],
                                  convw_c[:, e2, kk:kk + 1], xc[:, e2, :],
                                  OP.mult, OP.add, deps=[c0])
                    cops.append(c0)
                uops = [act(ub[:, e2, :], xc[:, e2, :], AF.Silu,
                            bias=convb_c[:, e2:e2 + 1], deps=[cops[e2]])
                        for e2 in range(E2)]
                if k % (NCH // 2) != NCH // 2 - 1:
                    vcopy(xi[:, :, 0:3], xi[:, :, TCP:TCP + 3], deps=cops)
                dops = []
                for tf in range(TCP // 512):
                    wop = mm(ps_a, [(xpwT[:, e2, :], ub[:, e2, tf * 512:(tf + 1) * 512])
                                    for e2 in range(E2)],
                             out_ap=ps_a.ap[0:96, 0:512], deps=uops)
                    dops.append(act(dbcc[0:96, tf * 512:(tf + 1) * 512],
                                    ps_a.ap[0:96, 0:512], AF.Copy, src_bank=ps_a))
                dma("g", u_st.rearrange("(e pp) T -> pp e T", pp=128)[:, :, k * TCP:(k + 1) * TCP],
                    ub[:, :, :], deps=uops)
                dma("g", zg_st.rearrange("(e pp) T -> pp e T", pp=128)[:, :, k * TCP:(k + 1) * TCP],
                    zgb[:, :, :], deps=ev_zg)
                dma("g", dbc_i[:, k * TCP:(k + 1) * TCP], dbcc[0:96, 0:TCP], deps=dops)
                p.barrier()

            ccr = coll("AllReduce", OP.add, [list(range(8))], dbc_i, dbc_o,
                       deps=[p.last["g"]])
            p.barrier()

            A.reset(ckpt2)
            lb = A.al(E2, TCP)
            lub = A.al(E2, TCP)
            bct = A.al(TCP)
            _op0 = A.off
            dab = [A.al(TCP), A.al(TCP)]
            dxb = A.al(TCP)
            hb = A.al(TCP)
            tmp_p = dxb
            g1b = hb
            yacc = [smalls[:, 0:TCP], smalls[:, TCP:2 * TCP]]
            ogb = A.al(E2, TCP)
            opst = arena[:, _op0:_op0 + DT * 512].rearrange("p (a b) -> p a b", a=DT)
            carry = carry_c.rearrange("p (e n) -> p e n", e=E2)
            for k in range(NCH):
                rk, toff = k // 2, (k % 2) * TCP
                l_u = dma("y", ub[:, :, :],
                          u_st.rearrange("(e pp) T -> pp e T", pp=128)[:, :, k * TCP:(k + 1) * TCP])
                l_z = dma("y", zgb[:, :, :],
                          zg_st.rearrange("(e pp) T -> pp e T", pp=128)[:, :, k * TCP:(k + 1) * TCP])
                l_d = dma("y", dbcc[0:96, 0:TCP], dbc_o[:, k * TCP:(k + 1) * TCP],
                          deps=[ccr])
                l_bc = dma("y", bct[0:32, 0:TCP], dbc_o[64:96, k * TCP:(k + 1) * TCP],
                           deps=[ccr])
                ld4 = [l_u, l_z, l_d, l_bc]
                lops = []
                for e2 in range(E2):
                    for tf in range(TCP // 512):
                        bk = ps_w if tf % 2 == 0 else ps_w2
                        wop = mm(bk, [(dtwT[0:RDT, e2 * 128:(e2 + 1) * 128],
                                       dbcc[0:RDT, tf * 512:(tf + 1) * 512])],
                                 out_ap=bk.ap[:, 0:512], deps=ld4)
                        lops.append(act(lb[:, e2, tf * 512:(tf + 1) * 512],
                                        bk.ap[:, 0:512], AF.Abs,
                                        bias=dtb_raw[:, e2:e2 + 1], src_bank=bk))
                        lops.append(act(xc[:, e2, tf * 512:(tf + 1) * 512],
                                        bk.ap[:, 0:512], AF.Relu,
                                        bias=dtb_raw[:, e2:e2 + 1], src_bank=bk))
                ee = act(lb.rearrange("p e t -> p (e t)"),
                         lb.rearrange("p e t -> p (e t)"), AF.Exp, scale=-1.0,
                         deps=lops)
                ll = ee
                tmpd = arena[:, _op0:_op0 + TCP]
                ltd = arena[:, _op0 + 2 * TCP:_op0 + 3 * TCP]
                for e2 in range(E2):
                    t1 = vts(tmpd, lb[:, e2, :], 1.0, OP.add, deps=[ee])
                    t2 = act(ltd, tmpd, AF.Ln, deps=[t1])
                    t3 = vstt(tmpd, tmpd, -1.0, lb[:, e2, :], OP.mult, OP.add,
                              deps=[t2])
                    t4 = vts(tmpd, tmpd, 1.0, OP.add, deps=[t3])
                    t5 = vtt(xc[:, e2, :], xc[:, e2, :], ltd, OP.add, deps=[t2, t4])
                    ll = vtt(lb[:, e2, :], xc[:, e2, :], tmpd, OP.add, deps=[t5])
                lu_ = vtt(lub.rearrange("p e t -> p (e t)"),
                          lb.rearrange("p e t -> p (e t)"),
                          ub.rearrange("p e t -> p (e t)"), OP.mult,
                          deps=[ll] + ld4)
                gops = []
                last_scan = [None, None]
                last_add = [[None, None], [None, None]]
                pbB = [ps_a, ps_c, ps_d]
                pbC = [ps_b, ps_e, ps_f]
                for n in range(NST):
                    brep = []
                    crep = []
                    for tf in range(TCP // 512):
                        r = (n * 2 + tf) % 3
                        bcs = bct[0:32, tf * 512:(tf + 1) * 512]
                        mm(pbB[r], [(selt[0:32, n, :], bcs)],
                           out_ap=pbB[r].ap[:, 0:512], deps=ld4)
                        brep.append(pbB[r])
                        mm(pbC[r], [(selt[0:32, NST + n, :], bcs)],
                           out_ap=pbC[r].ap[:, 0:512], deps=ld4)
                        crep.append(pbC[r])
                    for e2 in range(E2):
                        da_op = act(dab[e2][:, 0:TCP], lb[:, e2, :], AF.Exp,
                                    scale=pA[:, e2, n:n + 1],
                                    deps=[lu_, pa1, last_scan[e2]])
                        dx_ops = []
                        for tf in range(TCP // 512):
                            bk = brep[tf]
                            dx_ops.append(vtt(dxb[:, tf * 512:(tf + 1) * 512],
                                              lub[:, e2, tf * 512:(tf + 1) * 512],
                                              bk.ap[:, 0:512], OP.mult,
                                              deps=[lu_], banks=[bk]))
                        init = zeros_col[:, 0:1] if k % (NCH // 2) == 0 \
                            else carry[:, e2, n:n + 1]
                        sc = vscan(hb[:, 0:TCP], dab[e2][:, 0:TCP], dxb[:, 0:TCP],
                                   init, deps=[da_op] + dx_ops)
                        last_scan[e2] = sc
                        for tf in range(TCP // 512):
                            bk = crep[tf]
                            tgt = yacc[e2] if n == 0 else tmp_p
                            pr = vtt(tgt[:, tf * 512:(tf + 1) * 512],
                                     hb[:, tf * 512:(tf + 1) * 512],
                                     bk.ap[:, 0:512], OP.mult,
                                     deps=[sc], banks=[bk])
                            if tf == 0:
                                vcopy(carry[:, e2, n:n + 1],
                                      hb[:, TCP - 1:TCP], deps=[pr])
                            if n == 0:
                                last_add[e2][tf] = pr
                            else:
                                last_add[e2][tf] = vtt(
                                    yacc[e2][:, tf * 512:(tf + 1) * 512],
                                    yacc[e2][:, tf * 512:(tf + 1) * 512],
                                    tmp_p[:, tf * 512:(tf + 1) * 512],
                                    OP.add, deps=[pr])
                for e2 in range(E2):
                    for tf in range(TCP // 512):
                        g1 = vstt(g1b[:, tf * 512:(tf + 1) * 512],
                                  ub[:, e2, tf * 512:(tf + 1) * 512],
                                  dp_c[:, e2:e2 + 1],
                                  yacc[e2][:, tf * 512:(tf + 1) * 512],
                                  OP.mult, OP.add, deps=[last_add[e2][tf]] + ld4)
                        gops.append(vtt(ogb[:, e2, tf * 512:(tf + 1) * 512],
                                        g1b[:, tf * 512:(tf + 1) * 512],
                                        zgb[:, e2, tf * 512:(tf + 1) * 512],
                                        OP.mult, deps=[g1] + ld4))
                for tf in range(TCP // 512):
                    evs = []
                    for m in range(DT):
                        bk = ps_w if m % 2 == 0 else ps_w2
                        wop = mm(bk, [(outwT[:, e2, m * 128:(m + 1) * 128],
                                       ogb[:, e2, tf * 512:(tf + 1) * 512])
                                      for e2 in range(E2)],
                                 out_ap=bk.ap[:, 0:512], deps=gops)
                        evs.append(act(opst[:, m, :], bk.ap[:, 0:512], AF.Copy,
                                       src_bank=bk))
                    for m in range(DT):
                        dma("g", op_i[rk * D + m * 128: rk * D + (m + 1) * 128,
                                      toff + tf * 512: toff + (tf + 1) * 512],
                            opst[:, m, :], deps=[evs[m]])
                p.barrier()

            ccs = coll("ReduceScatter", OP.add, [list(range(8))], op_i, op_o,
                       deps=[p.last["g"]])
            p.barrier()
            A.reset(ckpt2)
            radd = A.al(DT, TCP)
            for ck in range(TLOC // TCP):
                t0 = ck * TCP
                l1 = [dma("y", radd[:, dt, :], op_o[dt * 128:(dt + 1) * 128, t0:t0 + TCP],
                          deps=[ccs]) for dt in range(DT)]
                for dt in range(DT):
                    vtt(resid[:, dt, t0:t0 + TCP], resid[:, dt, t0:t0 + TCP],
                        radd[:, dt, :], OP.add, deps=l1)
                p.barrier()

        ffn_phase()
        p.barrier()
        attn_entry()
        p.barrier()
        mamba_layer(0)
        p.barrier()
        mamba_layer(1)
        p.barrier()
        A.reset(0)
        tmp3 = A.al(DT, TCP)
        for ck in range(TLOC // TCP):
            t0 = ck * TCP
            l1 = [dma("y", tmp3[:, dt, :], xstash[dt * 128:(dt + 1) * 128, t0:t0 + TCP])
                  for dt in range(DT)]
            for dt in range(DT):
                vtt(resid[:, dt, t0:t0 + TCP], resid[:, dt, t0:t0 + TCP],
                    tmp3[:, dt, :], OP.add, deps=l1)
            p.barrier()
        ffn_phase()
        p.barrier()
        for dt in range(DT):
            dma("g", out_d[dt * 128:(dt + 1) * 128, :], resid[:, dt, :])

        p.finalize()
        hw = {e: {e2: 0 for e2 in ENGS} for e in ENGS}

        def emit_for(ename, eng):
            for o in p.ops:
                if o.eng != ename:
                    continue
                for d in o.deps:
                    if d.eng == ename and not (d.dma or d.acoll):
                        continue
                    if d.val is None:
                        continue
                    if d.val > hw[ename][d.eng]:
                        eng.wait_ge(sems[d.eng], d.val)
                        hw[ename][d.eng] = d.val
                inst = o.emit(eng)
                if o.need_inc:
                    inst.then_inc(sems[o.eng], 16 if o.dma else 1)
            if ename == "g":
                eng.wait_ge(sems["g"], p.totals["g"])

        @block.sync
        def _(sync):
            emit_for("y", sync)

        @block.gpsimd
        def _(gpsimd):
            emit_for("g", gpsimd)

        @block.vector
        def _(vector):
            emit_for("v", vector)

        @block.scalar
        def _(scalar):
            emit_for("s", scalar)

        @block.tensor
        def _(tensor):
            emit_for("t", tensor)

    return nc


_NC_CACHE = [None]


def kernel(**inp):
    f32 = np.float32
    cc = np.ascontiguousarray
    if _NC_CACHE[0] is None:
        _NC_CACHE[0] = build_nc()
    nc = _NC_CACHE[0]
    ident = np.eye(128, dtype=f32)
    sel = np.zeros((128, 2 * NST * 128), dtype=f32)
    for n in range(NST):
        sel[n, n * 128:(n + 1) * 128] = 1.0
        sel[NST + n, (NST + n) * 128:(NST + n + 1) * 128] = 1.0
    x = np.asarray(inp["L0_attn_x"], dtype=f32)
    in_maps = []
    for c in range(8):
        b, q = c // 4, c % 4
        s0, s1 = ESL * c, ESL * (c + 1)
        m = {
            "x_dmaj": cc(x[b, q * TLOC:(q + 1) * TLOC, :].T),
            "ident_in": ident,
            "sel_in": sel,
            "cnw1": cc(np.asarray(inp["L0_ffn_cn_w1"], f32)),
            "cnb1": cc(np.asarray(inp["L0_ffn_cn_b1"], f32)[:, None]),
            "cnw2": cc(np.asarray(inp["L0_ffn_cn_w2"], f32)),
            "cnb2": cc(np.asarray(inp["L0_ffn_cn_b2"], f32)[:, None]),
            "cnw3": cc(np.asarray(inp["L0_ffn_cn_w3"], f32)),
            "cnb3": cc(np.asarray(inp["L0_ffn_cn_b3"], f32)[:, None]),
            "cnw4": cc(np.asarray(inp["L0_ffn_cn_w4"], f32)),
            "cnb4": cc(np.asarray(inp["L0_ffn_cn_b4"], f32)[:, None]),
            "gcw1": cc(np.asarray(inp["L0_ffn_gc_w1"], f32)),
            "gcb1": cc(np.asarray(inp["L0_ffn_gc_b1"], f32)[:, None]),
            "gcw2": cc(np.asarray(inp["L0_ffn_gc_w2"], f32)),
            "gcb2": cc(np.asarray(inp["L0_ffn_gc_b2"], f32)[:, None]),
            "gcw3": cc(np.asarray(inp["L0_ffn_gc_w3"], f32)),
            "gcb3": cc(np.asarray(inp["L0_ffn_gc_b3"], f32)[:, None]),
            "ar": cc(np.asarray(inp["L0_ffn_adaptive_range"], f32)[:, None]),
        }
        for i in range(2):
            inw = np.asarray(inp["L0_attn_in_proj_w"], f32)[i]
            rows = np.concatenate([inw[s0:s1, :], inw[ED + s0:ED + s1, :]], axis=0)
            m[f"l{i}_inwT"] = cc(rows.T)
            m[f"l{i}_outwT"] = cc(np.asarray(inp["L0_attn_out_proj_w"], f32)[i][:, s0:s1].T)
            m[f"l{i}_xpwT"] = cc(np.asarray(inp["L0_attn_x_proj_w"], f32)[i][:, s0:s1].T)
            m[f"l{i}_dtwT"] = cc(np.asarray(inp["L0_attn_dt_w"], f32)[i][s0:s1, :].T)
            cw = np.asarray(inp["L0_attn_conv_w"], f32)[i][s0:s1, 0, :]
            m[f"l{i}_convw"] = cc(cw.reshape(E2, 128, KCV).transpose(1, 0, 2).reshape(128, E2 * KCV))
            m[f"l{i}_convb"] = cc(np.asarray(inp["L0_attn_conv_b"], f32)[i][s0:s1].reshape(E2, 128).T)
            m[f"l{i}_dtb"] = cc(np.asarray(inp["L0_attn_dt_b"], f32)[i][s0:s1].reshape(E2, 128).T)
            al = np.asarray(inp["L0_attn_A_log"], f32)[i][s0:s1, :]
            m[f"l{i}_alog"] = cc(al.reshape(E2, 128, NST).transpose(1, 0, 2).reshape(128, E2 * NST))
            m[f"l{i}_dp"] = cc(np.asarray(inp["L0_attn_D_p"], f32)[i][s0:s1].reshape(E2, 128).T)
            m[f"l{i}_normw"] = cc(np.asarray(inp["L0_attn_norm_w"], f32)[i].reshape(DT, 128).T)
        in_maps.append(m)
    res = run_bass_kernel_spmd(nc, in_maps, core_ids=list(range(8)))
    out = np.empty((B, L, D), dtype=f32)
    for c in range(8):
        b, q = c // 4, c % 4
        out[b, q * TLOC:(q + 1) * TLOC, :] = np.asarray(res.results[c]["out_dmaj"]).T
    return out


# version 14
# speedup vs baseline: 1.5960x; 1.0379x over previous
import numpy as np
import concourse.bass as bass
import concourse.mybir as mybir
from concourse.bass_utils import run_bass_kernel_spmd
from concourse.alu_op_type import AluOpType as OP

F32 = mybir.dt.float32
AF = mybir.ActivationFunctionType

B, L, D, ED, NST, KCV, RDT = 2, 8192, 1024, 2048, 16, 4, 64
TLOC = L // 4
ESL = ED // 8
BT = B * L
DT = D // 128
E2 = ESL // 128
TCA = 512
TCP = 1024
NCH = BT // TCP
TCF = 256
EPS = 1e-5

ENGS = ("y", "g", "v", "s", "t")


class Op:
    __slots__ = ("eng", "emit", "deps", "dma", "need_inc", "val", "acoll")

    def __init__(self, eng, emit, deps, dma, acoll=False):
        self.eng, self.emit, self.deps, self.dma = eng, emit, list(deps), dma
        self.acoll = acoll
        self.need_inc = dma or acoll
        self.val = None


class Plan:
    def __init__(self):
        self.ops = []
        self.last = {e: None for e in ENGS}
        self.bdeps = {e: [] for e in ENGS}

    def op(self, eng, emit, deps=(), dma=False, acoll=False):
        o = Op(eng, emit, [d for d in deps if d is not None], dma, acoll)
        if self.bdeps[eng]:
            o.deps.extend(self.bdeps[eng])
            self.bdeps[eng] = []
        self.ops.append(o)
        self.last[eng] = o
        return o

    def barrier(self):
        snap = [o for o in self.last.values() if o is not None]
        for e in ENGS:
            self.bdeps[e] = list(snap)

    def finalize(self):
        for o in self.ops:
            for d in o.deps:
                if d.eng != o.eng or d.dma or d.acoll:
                    d.need_inc = True
        ctr = {e: 0 for e in ENGS}
        for o in self.ops:
            if o.need_inc:
                ctr[o.eng] += 16 if o.dma else 1
                o.val = ctr[o.eng]
        self.totals = ctr


class PB:

    def __init__(self, ap):
        self.ap = ap
        self.readers = []
        self.writer = None


def build_nc():
    nc = bass.Bass("TRN2", target_bir_lowering=False)

    x_in = nc.dram_tensor("x_dmaj", [D, TLOC], F32, kind="ExternalInput").ap()
    id_in = nc.dram_tensor("ident_in", [128, 128], F32, kind="ExternalInput").ap()
    sel_in = nc.dram_tensor("sel_in", [128, 2 * NST * 128], F32, kind="ExternalInput").ap()
    out_d = nc.dram_tensor("out_dmaj", [D, TLOC], F32, kind="ExternalOutput").ap()
    lwin = []
    for i in range(2):
        lwin.append({k: nc.dram_tensor(f"l{i}_{k}", shp, F32, kind="ExternalInput").ap()
                     for k, shp in [("inwT", [D, 2 * ESL]), ("outwT", [ESL, D]),
                                    ("xpwT", [ESL, 96]), ("dtwT", [RDT, ESL]),
                                    ("convw", [128, E2 * KCV]), ("convb", [128, E2]),
                                    ("dtb", [128, E2]), ("alog", [128, E2 * NST]),
                                    ("dp", [128, E2]), ("normw", [128, DT])]})
    fwin = {nm: nc.dram_tensor(nm, shp, F32, kind="ExternalInput").ap()
            for nm, shp in [("cnw1", [D, D]), ("cnb1", [D, 1]),
                            ("cnw2", [D, D // 2]), ("cnb2", [D // 2, 1]),
                            ("cnw3", [D // 2, D // 4]), ("cnb3", [D // 4, 1]),
                            ("cnw4", [D // 4, 1]), ("cnb4", [1, 1]),
                            ("gcw1", [D, D // 2]), ("gcb1", [D // 2, 1]),
                            ("gcw2", [D // 2, D // 4]), ("gcb2", [D // 4, 1]),
                            ("gcw3", [D // 4, 1]), ("gcb3", [1, 1]), ("ar", [1, 1])]}
    xn_loc_h = [nc.dram_tensor(f"xn_loc{t}", [D, TCP], F32, kind="Internal").ap()
                for t in range(2)]
    xn_all_h = [nc.dram_tensor(f"xn_all{t}", [8 * D, TCP], F32, kind="Internal",
                               addr_space="Shared").ap() for t in range(2)]
    u_st = nc.dram_tensor("u_st", [ESL, BT], F32, kind="Internal").ap()
    zg_st = nc.dram_tensor("zg_st", [ESL, BT], F32, kind="Internal").ap()
    dbc_ih = [nc.dram_tensor(f"dbc_i{t}", [96, BT // 2], F32, kind="Internal").ap()
              for t in range(2)]
    dbc_oh = [nc.dram_tensor(f"dbc_o{t}", [96, BT // 2], F32, kind="Internal",
                             addr_space="Shared").ap() for t in range(2)]
    op_i = nc.dram_tensor("op_i", [8 * D, TLOC], F32, kind="Internal").ap()
    op_o = nc.dram_tensor("op_o", [D, TLOC], F32, kind="Internal").ap()
    gf_i = nc.dram_tensor("gf_i", [128, DT], F32, kind="Internal").ap()
    gf_o = nc.dram_tensor("gf_o", [128, DT], F32, kind="Internal").ap()
    xstash = nc.dram_tensor("xstash", [D, TLOC], F32, kind="Internal").ap()

    from contextlib import ExitStack
    with ExitStack() as stk:
        ec = stk.enter_context
        resid = ec(nc.sbuf_tensor("resid", [128, DT, TLOC], F32))
        smalls = ec(nc.sbuf_tensor("smalls", [128, 2048], F32))
        consts = ec(nc.sbuf_tensor("consts", [128, 512], F32))
        ident = ec(nc.sbuf_tensor("ident", [128, 128], F32))
        ones_t = ec(nc.sbuf_tensor("ones_t", [128, 128], F32))
        arena = ec(nc.sbuf_tensor("arena", [128, 29696], F32))
        pt_h = [ec(nc.psum_tensor(f"pt{i}", [128, 1024], F32)) for i in range(4)]
        ps_w = PB(pt_h[0][:, 0:512])
        ps_w2 = PB(pt_h[1][:, 0:512])
        ps_a = PB(pt_h[2][:, 0:512])
        ps_b = PB(pt_h[3][:, 0:512])
        tiles = [PB(pt_h[i][:, :]) for i in range(4)]
        sems = {e: ec(nc.semaphore(f"sem_{e}")) for e in ENGS}
        block = ec(nc.Block())

        p = Plan()

        c_off = [0]

        def calloc(n):
            o = c_off[0]
            c_off[0] += n
            assert c_off[0] <= 256
            return consts[:, o:o + n]

        pA = calloc(E2 * NST).rearrange("p (e n) -> p e n", e=E2)
        pA_raw = calloc(E2 * NST)
        dtb_raw = calloc(E2)
        convw_c = calloc(E2 * KCV).rearrange("p (e k) -> p e k", e=E2)
        convb_c = calloc(E2)
        dp_c = calloc(E2)
        normw_c = calloc(DT)
        ones_col = calloc(1)
        zeros_col = calloc(1)
        ones_row = ones_t[0:1, :]
        k0t = calloc(1)
        k1t = calloc(1)
        art = calloc(1)
        gft = calloc(1)
        carry_c = calloc(E2 * NST)
        _cmap = {}

        def constf(v):
            if v not in _cmap:
                _cmap[v] = calloc(1)
                _cinit.append(vmemset(_cmap[v], float(v)))
            return _cmap[v]
        _cinit = []

        class Ar:
            def __init__(self):
                self.off = 0

            def al(self, *shape):
                n = int(np.prod(shape))
                o = self.off
                self.off += n
                assert self.off <= 29696, f"arena overflow {self.off}"
                v = arena[:, o:o + n]
                if len(shape) == 2:
                    v = v.rearrange("p (a b) -> p a b", a=shape[0])
                return v

            def reset(self, off=0):
                self.off = off

        A = Ar()

        def dma(q, out, in_, deps=()):
            return p.op(q, lambda e, o=out, i=in_: e.dma_start(out=o, in_=i),
                        deps=deps, dma=True)

        def mm(bank, pairs, out_ap=None, deps=(), start=True, stop=True):
            tgt = out_ap if out_ap is not None else bank.ap[:]
            ops = []
            nmm = len(pairs)
            for j, (lh, rh) in enumerate(pairs):
                d = list(deps) + (bank.readers if (j == 0 and start) else []) \
                    + ([bank.writer] if (j == 0 and not start and bank.writer) else [])
                st = start and j == 0
                sp = stop and j == nmm - 1
                ops.append(p.op("t", lambda e, lh=lh, rh=rh, st=st, sp=sp, t=tgt:
                                e.matmul(t, lh, rh, start=st, stop=sp), deps=d))
            if start:
                bank.readers = []
            bank.writer = ops[-1]
            return ops[-1]

        def rd(bank, op):
            bank.readers.append(op)
            return op

        def act(out, in_, func, deps=(), src_bank=None, **kw):
            d = list(deps) + ([src_bank.writer] if src_bank else [])
            o = p.op("s", lambda e, o=out, i=in_, f=func, k=dict(kw):
                     e.activation(out=o, in_=i, func=f, **k), deps=d)
            if src_bank:
                rd(src_bank, o)
            return o

        def vtt(out, a, b, op_, deps=(), banks=()):
            d = list(deps) + [bk.writer for bk in banks if bk.writer]
            o = p.op("v", lambda e, o=out, x=a, y2=b, z=op_:
                     e.tensor_tensor(o, x, y2, z), deps=d)
            for bk in banks:
                rd(bk, o)
            return o

        def vts(out, a, s1, op_, deps=(), banks=()):
            d = list(deps) + [bk.writer for bk in banks if bk.writer]
            o = p.op("v", lambda e, o=out, x=a, s=s1, z=op_:
                     e.tensor_scalar(o, x, s, None, z), deps=d)
            for bk in banks:
                rd(bk, o)
            return o

        def vstt(out, in0, scal, in1, op0, op1, deps=(), banks=(), accum_out=None):
            d = list(deps) + [bk.writer for bk in banks if bk.writer]

            def f(e, o=out, i0=in0, sc=scal, i1=in1, a=op0, b2=op1, ac=accum_out):
                if ac is not None:
                    return e.scalar_tensor_tensor(o, i0, sc, i1, a, b2, accum_out=ac)
                return e.scalar_tensor_tensor(o, i0, sc, i1, a, b2)
            o = p.op("v", f, deps=d)
            for bk in banks:
                rd(bk, o)
            return o

        def vscan(out, d0, d1, init, deps=()):
            return p.op("v", lambda e, o=out, a=d0, b2=d1, i=init:
                        e.tensor_tensor_scan(o, a, b2, i, OP.mult, OP.add), deps=deps)

        def vcopy(out, in_, deps=()):
            return p.op("v", lambda e, o=out, i=in_: e.tensor_copy(o, i), deps=deps)

        def vred(out, in_, deps=()):
            return p.op("v", lambda e, o=out, i=in_:
                        e.tensor_reduce(out=o, in_=i, axis=mybir.AxisListType.X,
                                        op=OP.add), deps=deps)

        def vmemset(ap, val, deps=()):
            return p.op("v", lambda e, a=ap, v=val: e.memset(a, v), deps=deps)

        def coll(kind, op_, groups, in_ap, out_ap, deps=()):
            return p.op("g", lambda e, k=kind, o=op_, g=groups, i=in_ap, u2=out_ap:
                        e.collective_compute(k, o, replica_groups=g,
                                             ins=[i.opt()], outs=[u2.opt()]),
                        deps=deps, acoll=True)

        for dt in range(DT):
            dma("y", resid[:, dt, :], x_in[dt * 128:(dt + 1) * 128, :])
        dma("y", ident[:, :], id_in)
        vmemset(ones_col, 1.0)
        vmemset(ones_t[:, :], 1.0)
        vmemset(zeros_col, 0.0)
        dma("y", art[0:1, :], fwin["ar"])
        for _v in (EPS, -0.5, 1.0 / D):
            constf(_v)
        p.barrier()

        def ln_chunk(t0, tc, sqbuf, t1buf, out_buf, gf_ck=None, gfparts=None,
                     extra_dep_per_dt=None):
            sq_ops = [act(sqbuf[:, dt, 0:tc], resid[:, dt, t0:t0 + tc], AF.Square)
                      for dt in range(DT)]
            w_sx = mm(ps_w, [(ones_col, resid[:, dt, t0:t0 + tc]) for dt in range(DT)],
                      out_ap=ps_w.ap[0:1, 0:tc])
            w_sx2 = mm(ps_w2, [(ones_col, sqbuf[:, dt, 0:tc]) for dt in range(DT)],
                       out_ap=ps_w2.ap[0:1, 0:tc], deps=sq_ops)
            mu = smalls[0:1, 0:tc]
            ex2 = smalls[0:1, 512:512 + tc]
            var = smalls[0:1, 1024:1024 + tc]
            rstd = smalls[0:1, 1536:1536 + tc]
            e1 = vts(mu, ps_w.ap[0:1, 0:tc], 1.0 / D, OP.mult, banks=[ps_w])
            e2_ = vts(ex2, ps_w2.ap[0:1, 0:tc], 1.0 / D, OP.mult, banks=[ps_w2])
            e3 = vtt(var, mu, mu, OP.mult, deps=[e1])
            e4 = vtt(var, ex2, var, OP.subtract, deps=[e2_, e3])
            e5a = act(rstd, var, AF.Ln, bias=constf(EPS)[0:1, 0:1], deps=[e4])
            e5 = act(rstd, rstd, AF.Exp, scale=constf(-0.5)[0:1, 0:1], deps=[e5a])
            r1 = mm(ps_a, [(ones_row, mu)], out_ap=ps_a.ap[:, 0:tc],
                    deps=[e3])
            r2 = mm(ps_b, [(ones_row, rstd)], out_ap=ps_b.ap[:, 0:tc],
                    deps=[e5])
            outs = []
            for dt in range(DT):
                xd = [extra_dep_per_dt[dt]] if extra_dep_per_dt else []
                s1 = vtt(t1buf[:, dt, 0:tc], resid[:, dt, t0:t0 + tc], ps_a.ap[:, 0:tc],
                         OP.subtract, deps=[sq_ops[dt]] + xd, banks=[ps_a])
                ac = gfparts[:, dt, gf_ck:gf_ck + 1] if gf_ck is not None else None
                outs.append(vstt(out_buf[:, dt, 0:tc], t1buf[:, dt, 0:tc], 1.0,
                                 ps_b.ap[:, 0:tc], OP.bypass, OP.mult,
                                 deps=[s1], banks=[ps_b], accum_out=ac))
            return outs

        def ffn_phase():
            A.reset(0)
            w1 = A.al(DT, D)
            w2 = A.al(DT, D // 2)
            w3 = A.al(4, D // 4)
            w4 = A.al(2, 1)
            g1w = A.al(DT, D // 2)
            g2w = A.al(4, D // 4)
            g3w = A.al(2, 1)
            b1 = A.al(DT)
            b2 = A.al(4)
            b3 = A.al(2)
            b4 = A.al(1)
            gb1 = A.al(4)
            gb2 = A.al(2)
            gb3 = A.al(1)
            lnb = A.al(DT, TCF)
            h1 = A.al(DT, TCF)
            h2 = A.al(4, TCF)
            h3 = A.al(2, TCF)
            tmp8 = A.al(DT, TCF)
            t1b = A.al(DT, TCF)
            gfparts = A.al(DT, DT)
            gfp = A.al(DT)

            for src, dst in [(fwin["cnw1"], w1), (fwin["cnw2"], w2), (fwin["cnw3"], w3),
                             (fwin["cnw4"], w4), (fwin["gcw1"], g1w), (fwin["gcw2"], g2w),
                             (fwin["gcw3"], g3w)]:
                for k in range(src.shape[0] // 128):
                    dma("y", dst[:, k, :], src[k * 128:(k + 1) * 128, :])
            for src, dst in [(fwin["cnb1"], b1), (fwin["cnb2"], b2), (fwin["cnb3"], b3),
                             (fwin["gcb1"], gb1), (fwin["gcb2"], gb2)]:
                for k in range(src.shape[0] // 128):
                    dma("y", dst[:, k:k + 1], src[k * 128:(k + 1) * 128, :])
            dma("y", b4[0:1, 0:1], fwin["cnb4"])
            dma("y", gb3[0:1, 0:1], fwin["gcb3"])
            p.barrier()

            nchf = TLOC // TCF
            for ck in range(nchf):
                ln_chunk(ck * TCF, TCF, tmp8, t1b, lnb, gf_ck=ck, gfparts=gfparts)
                p.barrier()
            rg = vred(gfp[:, 0:DT], gfparts)
            d1 = dma("g", gf_i, gfp[:, 0:DT], deps=[rg])
            cc1 = coll("AllReduce", OP.add, [[0, 1, 2, 3], [4, 5, 6, 7]],
                       gf_i, gf_o, deps=[d1])
            d2 = dma("y", gfp[:, 0:DT], gf_o, deps=[cc1])
            mlv = vts(gfp[:, 0:DT], gfp[:, 0:DT], 1.0 / L, OP.mult, deps=[d2])

            gh1 = smalls[0:128, 0:4]
            gh2 = smalls[0:128, 4:6]
            lastb1 = []
            for m in range(4):
                wop = mm(ps_w, [(g1w[:, k, m * 128:(m + 1) * 128], gfp[:, k:k + 1])
                                for k in range(DT)],
                         out_ap=ps_w.ap[:, 0:1], deps=[mlv])
                lastb1.append(act(gh1[:, m:m + 1], ps_w.ap[:, 0:1], AF.Gelu,
                                  bias=gb1[:, m:m + 1], src_bank=ps_w))
            l2 = []
            for m in range(2):
                wop = mm(ps_w2, [(g2w[:, k, m * 128:(m + 1) * 128], gh1[:, k:k + 1])
                                 for k in range(4)],
                         out_ap=ps_w2.ap[:, 0:1], deps=lastb1)
                l2.append(act(gh2[:, m:m + 1], ps_w2.ap[:, 0:1], AF.Gelu,
                              bias=gb2[:, m:m + 1], src_bank=ps_w2))
            wop = mm(ps_w, [(g3w[:, k, 0:1], gh2[:, k:k + 1]) for k in range(2)],
                     out_ap=ps_w.ap[0:1, 0:1], deps=l2)
            gfs = act(gft[0:1, :], ps_w.ap[0:1, 0:1], AF.Sigmoid,
                      bias=gb3[0:1, 0:1], src_bank=ps_w)
            kk1 = vts(k1t[0:1, :], art[0:1, :], 0.6, OP.mult)
            t1_ = act(k0t[0:1, :], gft[0:1, :], AF.Copy, bias=-0.5, scale=0.7,
                      deps=[gfs])
            t2_ = vtt(k0t[0:1, :], k0t[0:1, :], art[0:1, :], OP.mult, deps=[t1_])
            t3_ = act(k0t[0:1, :], k0t[0:1, :], AF.Copy, bias=0.1, scale=2.0,
                      deps=[t2_])
            p.barrier()

            TW_SRC = [4, 5, 6, 7, 0, 1, 2, 3]
            TW_SGN = [+1, +1, -1, -1, -1, -1, +1, +1]
            for ck in range(nchf):
                t0 = ck * TCF
                lnb_w = ln_chunk(t0, TCF, tmp8, t1b, lnb)
                for cyc in range(3):
                    h1ops = []
                    for m in range(DT):
                        bk = ps_w if m % 2 == 0 else ps_w2
                        wop = mm(bk, [(w1[:, k, m * 128:(m + 1) * 128], lnb[:, k, :])
                                      for k in range(DT)], out_ap=bk.ap[:, 0:TCF],
                                 deps=lnb_w)
                        h1ops.append(act(h1[:, m, :], bk.ap[:, 0:TCF], AF.Gelu,
                                         bias=b1[:, m:m + 1], src_bank=bk))
                    h2ops = []
                    for m in range(4):
                        bk = ps_w if m % 2 == 0 else ps_w2
                        wop = mm(bk, [(w2[:, k, m * 128:(m + 1) * 128], h1[:, k, :])
                                      for k in range(DT)], out_ap=bk.ap[:, 0:TCF],
                                 deps=h1ops)
                        h2ops.append(act(h2[:, m, :], bk.ap[:, 0:TCF], AF.Gelu,
                                         bias=b2[:, m:m + 1], src_bank=bk))
                    h3ops = []
                    for m in range(2):
                        bk = ps_w if m % 2 == 0 else ps_w2
                        wop = mm(bk, [(w3[:, k, m * 128:(m + 1) * 128], h2[:, k, :])
                                      for k in range(4)], out_ap=bk.ap[:, 0:TCF],
                                 deps=h2ops)
                        h3ops.append(act(h3[:, m, :], bk.ap[:, 0:TCF], AF.Gelu,
                                         bias=b3[:, m:m + 1], src_bank=bk))
                    wop = mm(ps_w2, [(w4[:, k, :], h3[:, k, :]) for k in range(2)],
                             out_ap=ps_w2.ap[0:1, 0:TCF], deps=h3ops)
                    tf = smalls[0:1, 512:512 + TCF]
                    cpl = smalls[0:1, 1024:1024 + TCF]
                    a1 = act(tf, ps_w2.ap[0:1, 0:TCF], AF.Sigmoid,
                             bias=b4[0:1, 0:1], src_bank=ps_w2)
                    a2 = p.op("v", lambda e, o=cpl, i=tf: e.tensor_scalar(
                        o, i, k1t[0:1, 0:1], k0t[0:1, 0:1], OP.mult, OP.add),
                        deps=[a1, kk1, t3_])
                    rr = mm(ps_a, [(ones_row, cpl)],
                            out_ap=ps_a.ap[:, 0:TCF], deps=[a2])
                    tops = []
                    for dt in range(DT):
                        tops.append(vtt(tmp8[:, dt, :], lnb[:, TW_SRC[dt], :],
                                        ps_a.ap[:, 0:TCF], OP.mult,
                                        deps=lnb_w, banks=[ps_a]))
                    lnb_w = []
                    for dt in range(DT):
                        lnb_w.append(vtt(lnb[:, dt, :], lnb[:, dt, :], tmp8[:, dt, :],
                                         OP.add if TW_SGN[dt] > 0 else OP.subtract,
                                         deps=tops))
                for dt in range(DT):
                    vstt(resid[:, dt, t0:t0 + TCF], lnb[:, dt, :], 0.5,
                         resid[:, dt, t0:t0 + TCF], OP.mult, OP.add, deps=lnb_w)
                p.barrier()

        def attn_entry():
            A.reset(0)
            tmp = A.al(DT, TCA)
            sqt = A.al(DT, TCA)
            stash = [dma("g", xstash[dt * 128:(dt + 1) * 128, :], resid[:, dt, :])
                     for dt in range(DT)]
            for ck in range(TLOC // TCA):
                ln_chunk_into_resid(ck * TCA, TCA, sqt, tmp, stash)
                p.barrier()

        def ln_chunk_into_resid(t0, tc, sqbuf, t1buf, stash):
            sq_ops = [act(sqbuf[:, dt, 0:tc], resid[:, dt, t0:t0 + tc], AF.Square)
                      for dt in range(DT)]
            w_sx = mm(ps_w, [(ones_col, resid[:, dt, t0:t0 + tc]) for dt in range(DT)],
                      out_ap=ps_w.ap[0:1, 0:tc])
            w_sx2 = mm(ps_w2, [(ones_col, sqbuf[:, dt, 0:tc]) for dt in range(DT)],
                       out_ap=ps_w2.ap[0:1, 0:tc], deps=sq_ops)
            mu = smalls[0:1, 0:tc]
            ex2 = smalls[0:1, 512:512 + tc]
            var = smalls[0:1, 1024:1024 + tc]
            rstd = smalls[0:1, 1536:1536 + tc]
            e1 = vts(mu, ps_w.ap[0:1, 0:tc], 1.0 / D, OP.mult, banks=[ps_w])
            e2_ = vts(ex2, ps_w2.ap[0:1, 0:tc], 1.0 / D, OP.mult, banks=[ps_w2])
            e3 = vtt(var, mu, mu, OP.mult, deps=[e1])
            e4 = vtt(var, ex2, var, OP.subtract, deps=[e2_, e3])
            e5a = act(rstd, var, AF.Ln, bias=constf(EPS)[0:1, 0:1], deps=[e4])
            e5 = act(rstd, rstd, AF.Exp, scale=constf(-0.5)[0:1, 0:1], deps=[e5a])
            r1 = mm(ps_a, [(ones_row, mu)], out_ap=ps_a.ap[:, 0:tc], deps=[e3])
            r2 = mm(ps_b, [(ones_row, rstd)], out_ap=ps_b.ap[:, 0:tc], deps=[e5])
            outs = []
            for dt in range(DT):
                s1 = vtt(t1buf[:, dt, 0:tc], resid[:, dt, t0:t0 + tc], ps_a.ap[:, 0:tc],
                         OP.subtract, deps=[sq_ops[dt], w_sx] + stash, banks=[ps_a])
                outs.append(vstt(resid[:, dt, t0:t0 + tc], t1buf[:, dt, 0:tc], 1.0,
                                 ps_b.ap[:, 0:tc], OP.bypass, OP.mult,
                                 deps=[s1], banks=[ps_b]))
            return outs

        def mamba_layer(i):
            W = lwin[i]
            A.reset(0)
            selt = A.al(2 * NST, 128)
            inwT = A.al(DT, 2 * ESL)
            outwT = A.al(E2, D)
            xpwT = A.al(E2, 96)
            dtwT = A.al(ESL)
            wl = []
            for k in range(DT):
                wl.append(dma("y", inwT[:, k, :], W["inwT"][k * 128:(k + 1) * 128, :]))
            for k in range(E2):
                wl.append(dma("y", outwT[:, k, :], W["outwT"][k * 128:(k + 1) * 128, :]))
                wl.append(dma("y", xpwT[:, k, :], W["xpwT"][k * 128:(k + 1) * 128, :]))
            wl.append(dma("y", dtwT[0:RDT, :], W["dtwT"]))
            wl.append(dma("y", selt.rearrange("p a b -> p (a b)"), sel_in))
            wl.append(dma("y", convw_c.rearrange("p e k -> p (e k)"), W["convw"]))
            wl.append(dma("y", convb_c, W["convb"]))
            wl.append(dma("y", dtb_raw, W["dtb"]))
            wl.append(dma("y", pA_raw, W["alog"]))
            wl.append(dma("y", dp_c, W["dp"]))
            wl.append(dma("y", normw_c, W["normw"]))
            pa0 = act(pA.rearrange("p e n -> p (e n)"), pA_raw, AF.Exp, deps=wl)
            pa1 = vts(pA.rearrange("p e n -> p (e n)"), pA.rearrange("p e n -> p (e n)"),
                      -1.0, OP.mult, deps=[pa0])
            p.barrier()

            ub = A.al(E2, TCP)
            zgb = A.al(E2, TCP)
            xc = A.al(E2, TCP)
            dbcc = A.al(TCP)
            ckpt2 = A.off

            tmpa = A.al(DT, TCA)
            sqa = A.al(DT, TCA)
            for ck in range(TLOC // TCA):
                t0 = ck * TCA
                sq_ops = [act(sqa[:, dt, 0:TCA], resid[:, dt, t0:t0 + TCA], AF.Square)
                          for dt in range(DT)]
                w_sx2 = mm(ps_w, [(ones_col, sqa[:, dt, 0:TCA]) for dt in range(DT)],
                           out_ap=ps_w.ap[0:1, 0:TCA], deps=sq_ops)
                rstd = smalls[0:1, 0:TCA]
                e5a = act(rstd, ps_w.ap[0:1, 0:TCA], AF.Ln,
                          scale=constf(1.0 / D)[0:1, 0:1], bias=constf(EPS)[0:1, 0:1],
                          src_bank=ps_w)
                e5 = act(rstd, rstd, AF.Exp, scale=constf(-0.5)[0:1, 0:1], deps=[e5a])
                r2 = mm(ps_b, [(ones_row, rstd)], out_ap=ps_b.ap[:, 0:TCA],
                        deps=[e5])
                for dt in range(DT):
                    sop = vstt(tmpa[:, dt, 0:TCA], resid[:, dt, t0:t0 + TCA],
                               normw_c[:, dt:dt + 1], ps_b.ap[:, 0:TCA],
                               OP.mult, OP.mult, deps=[sq_ops[dt]], banks=[ps_b])
                    dma("g", xn_loc_h[ck // 2][dt * 128:(dt + 1) * 128,
                                               (ck % 2) * TCA:(ck % 2) * TCA + TCA],
                        tmpa[:, dt, 0:TCA], deps=[sop])
                p.barrier()
                if ck == 1:
                    ccg_h = [coll("AllGather", OP.bypass, [list(range(8))],
                                  xn_loc_h[0], xn_all_h[0], deps=[p.last["g"]])]
            ccg_h.append(coll("AllGather", OP.bypass, [list(range(8))],
                              xn_loc_h[1], xn_all_h[1], deps=[p.last["g"]]))
            p.op("g", lambda e: e.nop())
            p.barrier()

            A.reset(ckpt2)
            xn_in = A.al(DT, TCP)
            xi = A.al(E2, TCP + 3)
            ccr_h = []
            for k in range(NCH):
                rk = k // 2
                if k == NCH // 2:
                    ccr_h.append(coll("AllReduce", OP.add, [list(range(8))],
                                      dbc_ih[0], dbc_oh[0], deps=[p.last["g"]]))
                xl = [dma("y", xn_in[:, dt, :],
                          xn_all_h[k % 2][rk * D + dt * 128: rk * D + (dt + 1) * 128,
                                          0:TCP], deps=[ccg_h[k % 2]])
                      for dt in range(DT)]
                ev_xi = []
                ev_zg = []
                for m in range(4):
                    for tf in range(TCP // 512):
                        bk = ps_w if (m * 2 + tf) % 2 == 0 else ps_w2
                        wop = mm(bk, [(inwT[:, kk, m * 128:(m + 1) * 128],
                                       xn_in[:, kk, tf * 512:(tf + 1) * 512])
                                      for kk in range(DT)],
                                 out_ap=bk.ap[:, 0:512], deps=xl)
                        if m < 2:
                            ev_xi.append(act(xi[:, m, 3 + tf * 512: 3 + (tf + 1) * 512],
                                             bk.ap[:, 0:512], AF.Copy, src_bank=bk))
                        else:
                            ev_zg.append(act(zgb[:, m - 2, tf * 512:(tf + 1) * 512],
                                             bk.ap[:, 0:512], AF.Silu, src_bank=bk))
                halo = []
                if k % (NCH // 2) == 0:
                    halo = [vmemset(xi[:, :, 0:3], 0.0)]
                cops = []
                for e2 in range(E2):
                    c0 = vts(xc[:, e2, :], xi[:, e2, 0:TCP], convw_c[:, e2, 0:1],
                             OP.mult, deps=ev_xi + halo)
                    for kk in range(1, KCV):
                        c0 = vstt(xc[:, e2, :], xi[:, e2, kk:kk + TCP],
                                  convw_c[:, e2, kk:kk + 1], xc[:, e2, :],
                                  OP.mult, OP.add, deps=[c0])
                    cops.append(c0)
                uops = [act(ub[:, e2, :], xc[:, e2, :], AF.Silu,
                            bias=convb_c[:, e2:e2 + 1], deps=[cops[e2]])
                        for e2 in range(E2)]
                if k % (NCH // 2) != NCH // 2 - 1:
                    vcopy(xi[:, :, 0:3], xi[:, :, TCP:TCP + 3], deps=cops)
                dops = []
                for tf in range(TCP // 512):
                    wop = mm(ps_a, [(xpwT[:, e2, :], ub[:, e2, tf * 512:(tf + 1) * 512])
                                    for e2 in range(E2)],
                             out_ap=ps_a.ap[0:96, 0:512], deps=uops)
                    dops.append(act(dbcc[0:96, tf * 512:(tf + 1) * 512],
                                    ps_a.ap[0:96, 0:512], AF.Copy, src_bank=ps_a))
                dma("g", u_st.rearrange("(e pp) T -> pp e T", pp=128)[:, :, k * TCP:(k + 1) * TCP],
                    ub[:, :, :], deps=uops)
                dma("g", zg_st.rearrange("(e pp) T -> pp e T", pp=128)[:, :, k * TCP:(k + 1) * TCP],
                    zgb[:, :, :], deps=ev_zg)
                dma("g", dbc_ih[k // (NCH // 2)][:, (k % (NCH // 2)) * TCP:
                                                 (k % (NCH // 2) + 1) * TCP],
                    dbcc[0:96, 0:TCP], deps=dops)
                p.barrier()

            ccr_h.append(coll("AllReduce", OP.add, [list(range(8))],
                              dbc_ih[1], dbc_oh[1], deps=[p.last["g"]]))
            p.op("g", lambda e: e.nop())
            p.barrier()

            A.reset(ckpt2)
            lb = A.al(E2, TCP)
            lub = A.al(E2, TCP)
            bct = A.al(TCP)
            _op0 = A.off
            dab = [A.al(TCP), A.al(TCP)]
            dxb = A.al(TCP)
            hb = A.al(TCP)
            tmp_p = dxb
            g1b = hb
            yacc = [smalls[:, 0:TCP], smalls[:, TCP:2 * TCP]]
            ogb = A.al(E2, TCP)
            opst = arena[:, _op0:_op0 + DT * 512].rearrange("p (a b) -> p a b", a=DT)
            carry = carry_c.rearrange("p (e n) -> p e n", e=E2)
            for k in range(NCH):
                rk, toff = k // 2, (k % 2) * TCP
                l_u = dma("y", ub[:, :, :],
                          u_st.rearrange("(e pp) T -> pp e T", pp=128)[:, :, k * TCP:(k + 1) * TCP])
                l_z = dma("y", zgb[:, :, :],
                          zg_st.rearrange("(e pp) T -> pp e T", pp=128)[:, :, k * TCP:(k + 1) * TCP])
                kh, kc = k // (NCH // 2), (k % (NCH // 2)) * TCP
                l_d = dma("y", dbcc[0:96, 0:TCP], dbc_oh[kh][:, kc:kc + TCP],
                          deps=[ccr_h[kh]])
                l_bc = dma("y", bct[0:32, 0:TCP], dbc_oh[kh][64:96, kc:kc + TCP],
                           deps=[ccr_h[kh]])
                ld4 = [l_u, l_z, l_d, l_bc]
                lops = []
                for e2 in range(E2):
                    for tf in range(TCP // 512):
                        bk = tiles[e2 * 2 + tf]
                        wop = mm(bk, [(dtwT[0:RDT, e2 * 128:(e2 + 1) * 128],
                                       dbcc[0:RDT, tf * 512:(tf + 1) * 512])],
                                 out_ap=bk.ap[:, 0:512], deps=ld4)
                        lops.append(act(lb[:, e2, tf * 512:(tf + 1) * 512],
                                        bk.ap[:, 0:512], AF.Abs,
                                        bias=dtb_raw[:, e2:e2 + 1], src_bank=bk))
                        lops.append(act(xc[:, e2, tf * 512:(tf + 1) * 512],
                                        bk.ap[:, 0:512], AF.Relu,
                                        bias=dtb_raw[:, e2:e2 + 1], src_bank=bk))
                ee = act(lb.rearrange("p e t -> p (e t)"),
                         lb.rearrange("p e t -> p (e t)"), AF.Exp, scale=-1.0,
                         deps=lops)
                ll = ee
                tmpd = arena[:, _op0:_op0 + TCP]
                ltd = arena[:, _op0 + 2 * TCP:_op0 + 3 * TCP]
                for e2 in range(E2):
                    t1 = vts(tmpd, lb[:, e2, :], 1.0, OP.add, deps=[ee])
                    t2 = act(ltd, tmpd, AF.Ln, deps=[t1])
                    t3 = vstt(tmpd, tmpd, -1.0, lb[:, e2, :], OP.mult, OP.add,
                              deps=[t2])
                    t4 = vts(tmpd, tmpd, 1.0, OP.add, deps=[t3])
                    t5 = vtt(xc[:, e2, :], xc[:, e2, :], ltd, OP.add, deps=[t2, t4])
                    ll = vtt(lb[:, e2, :], xc[:, e2, :], tmpd, OP.add, deps=[t5])
                lu_ = vtt(lub.rearrange("p e t -> p (e t)"),
                          lb.rearrange("p e t -> p (e t)"),
                          ub.rearrange("p e t -> p (e t)"), OP.mult,
                          deps=[ll] + ld4)
                gops = []
                last_scan = [None, None]
                last_add = [None, None]
                for n in range(NST):
                    tb, tc = tiles[n % 2], tiles[2 + n % 2]
                    for tf in range(TCP // 512):
                        bcs = bct[0:32, tf * 512:(tf + 1) * 512]
                        mm(tb, [(selt[0:32, n, :], bcs)],
                           out_ap=tb.ap[:, tf * 512:(tf + 1) * 512], deps=ld4)
                        mm(tc, [(selt[0:32, NST + n, :], bcs)],
                           out_ap=tc.ap[:, tf * 512:(tf + 1) * 512], deps=ld4)
                    for e2 in range(E2):
                        da_op = act(dab[e2][:, 0:TCP], lb[:, e2, :], AF.Exp,
                                    scale=pA[:, e2, n:n + 1],
                                    deps=[lu_, pa1, last_scan[e2]])
                        dx = vtt(dxb[:, 0:TCP], lub[:, e2, :], tb.ap[:, 0:TCP],
                                 OP.mult, deps=[lu_], banks=[tb])
                        init = zeros_col[:, 0:1] if k % (NCH // 2) == 0 \
                            else carry[:, e2, n:n + 1]
                        sc = vscan(hb[:, 0:TCP], dab[e2][:, 0:TCP], dxb[:, 0:TCP],
                                   init, deps=[da_op, dx])
                        last_scan[e2] = sc
                        tgt = yacc[e2] if n == 0 else tmp_p
                        pr0 = vtt(tgt[:, 0:512], hb[:, 0:512], tc.ap[:, 0:512],
                                  OP.mult, deps=[sc], banks=[tc])
                        vcopy(carry[:, e2, n:n + 1], hb[:, TCP - 1:TCP],
                              deps=[pr0])
                        pr1 = vtt(tgt[:, 512:TCP], hb[:, 512:TCP],
                                  tc.ap[:, 512:TCP], OP.mult, deps=[sc],
                                  banks=[tc])
                        if n == 0:
                            last_add[e2] = pr1
                        else:
                            last_add[e2] = vtt(yacc[e2][:, 0:TCP],
                                               yacc[e2][:, 0:TCP],
                                               tmp_p[:, 0:TCP],
                                               OP.add, deps=[pr1])
                for e2 in range(E2):
                    g1 = vstt(g1b[:, 0:TCP], ub[:, e2, :], dp_c[:, e2:e2 + 1],
                              yacc[e2][:, 0:TCP],
                              OP.mult, OP.add, deps=[last_add[e2]] + ld4)
                    gops.append(vtt(ogb[:, e2, :], g1b[:, 0:TCP],
                                    zgb[:, e2, :], OP.mult, deps=[g1] + ld4))
                for tf in range(TCP // 512):
                    evs = []
                    for m in range(DT):
                        bk = ps_w if m % 2 == 0 else ps_w2
                        wop = mm(bk, [(outwT[:, e2, m * 128:(m + 1) * 128],
                                       ogb[:, e2, tf * 512:(tf + 1) * 512])
                                      for e2 in range(E2)],
                                 out_ap=bk.ap[:, 0:512], deps=gops)
                        evs.append(act(opst[:, m, :], bk.ap[:, 0:512], AF.Copy,
                                       src_bank=bk))
                    for m in range(DT):
                        dma("g", op_i[rk * D + m * 128: rk * D + (m + 1) * 128,
                                      toff + tf * 512: toff + (tf + 1) * 512],
                            opst[:, m, :], deps=[evs[m]])
                p.barrier()

            ccs = coll("ReduceScatter", OP.add, [list(range(8))], op_i, op_o,
                       deps=[p.last["g"]])
            p.barrier()
            A.reset(ckpt2)
            radd = A.al(DT, TCP)
            for ck in range(TLOC // TCP):
                t0 = ck * TCP
                l1 = [dma("y", radd[:, dt, :], op_o[dt * 128:(dt + 1) * 128, t0:t0 + TCP],
                          deps=[ccs]) for dt in range(DT)]
                for dt in range(DT):
                    vtt(resid[:, dt, t0:t0 + TCP], resid[:, dt, t0:t0 + TCP],
                        radd[:, dt, :], OP.add, deps=l1)
                p.barrier()

        ffn_phase()
        p.barrier()
        attn_entry()
        p.barrier()
        mamba_layer(0)
        p.barrier()
        mamba_layer(1)
        p.barrier()
        A.reset(0)
        tmp3 = A.al(DT, TCP)
        for ck in range(TLOC // TCP):
            t0 = ck * TCP
            l1 = [dma("y", tmp3[:, dt, :], xstash[dt * 128:(dt + 1) * 128, t0:t0 + TCP])
                  for dt in range(DT)]
            for dt in range(DT):
                vtt(resid[:, dt, t0:t0 + TCP], resid[:, dt, t0:t0 + TCP],
                    tmp3[:, dt, :], OP.add, deps=l1)
            p.barrier()
        ffn_phase()
        p.barrier()
        for dt in range(DT):
            dma("g", out_d[dt * 128:(dt + 1) * 128, :], resid[:, dt, :])

        p.finalize()
        hw = {e: {e2: 0 for e2 in ENGS} for e in ENGS}

        def emit_for(ename, eng):
            for o in p.ops:
                if o.eng != ename:
                    continue
                for d in o.deps:
                    if d.eng == ename and not (d.dma or d.acoll):
                        continue
                    if d.val is None:
                        continue
                    if d.val > hw[ename][d.eng]:
                        eng.wait_ge(sems[d.eng], d.val)
                        hw[ename][d.eng] = d.val
                inst = o.emit(eng)
                if o.need_inc:
                    inst.then_inc(sems[o.eng], 16 if o.dma else 1)
            if ename == "g":
                eng.wait_ge(sems["g"], p.totals["g"])

        @block.sync
        def _(sync):
            emit_for("y", sync)

        @block.gpsimd
        def _(gpsimd):
            emit_for("g", gpsimd)

        @block.vector
        def _(vector):
            emit_for("v", vector)

        @block.scalar
        def _(scalar):
            emit_for("s", scalar)

        @block.tensor
        def _(tensor):
            emit_for("t", tensor)

    return nc


_NC_CACHE = [None]


def kernel(**inp):
    f32 = np.float32
    cc = np.ascontiguousarray
    if _NC_CACHE[0] is None:
        _NC_CACHE[0] = build_nc()
    nc = _NC_CACHE[0]
    ident = np.eye(128, dtype=f32)
    sel = np.zeros((128, 2 * NST * 128), dtype=f32)
    for n in range(NST):
        sel[n, n * 128:(n + 1) * 128] = 1.0
        sel[NST + n, (NST + n) * 128:(NST + n + 1) * 128] = 1.0
    x = np.asarray(inp["L0_attn_x"], dtype=f32)
    in_maps = []
    for c in range(8):
        b, q = c // 4, c % 4
        s0, s1 = ESL * c, ESL * (c + 1)
        m = {
            "x_dmaj": cc(x[b, q * TLOC:(q + 1) * TLOC, :].T),
            "ident_in": ident,
            "sel_in": sel,
            "cnw1": cc(np.asarray(inp["L0_ffn_cn_w1"], f32)),
            "cnb1": cc(np.asarray(inp["L0_ffn_cn_b1"], f32)[:, None]),
            "cnw2": cc(np.asarray(inp["L0_ffn_cn_w2"], f32)),
            "cnb2": cc(np.asarray(inp["L0_ffn_cn_b2"], f32)[:, None]),
            "cnw3": cc(np.asarray(inp["L0_ffn_cn_w3"], f32)),
            "cnb3": cc(np.asarray(inp["L0_ffn_cn_b3"], f32)[:, None]),
            "cnw4": cc(np.asarray(inp["L0_ffn_cn_w4"], f32)),
            "cnb4": cc(np.asarray(inp["L0_ffn_cn_b4"], f32)[:, None]),
            "gcw1": cc(np.asarray(inp["L0_ffn_gc_w1"], f32)),
            "gcb1": cc(np.asarray(inp["L0_ffn_gc_b1"], f32)[:, None]),
            "gcw2": cc(np.asarray(inp["L0_ffn_gc_w2"], f32)),
            "gcb2": cc(np.asarray(inp["L0_ffn_gc_b2"], f32)[:, None]),
            "gcw3": cc(np.asarray(inp["L0_ffn_gc_w3"], f32)),
            "gcb3": cc(np.asarray(inp["L0_ffn_gc_b3"], f32)[:, None]),
            "ar": cc(np.asarray(inp["L0_ffn_adaptive_range"], f32)[:, None]),
        }
        for i in range(2):
            inw = np.asarray(inp["L0_attn_in_proj_w"], f32)[i]
            rows = np.concatenate([inw[s0:s1, :], inw[ED + s0:ED + s1, :]], axis=0)
            m[f"l{i}_inwT"] = cc(rows.T)
            m[f"l{i}_outwT"] = cc(np.asarray(inp["L0_attn_out_proj_w"], f32)[i][:, s0:s1].T)
            m[f"l{i}_xpwT"] = cc(np.asarray(inp["L0_attn_x_proj_w"], f32)[i][:, s0:s1].T)
            m[f"l{i}_dtwT"] = cc(np.asarray(inp["L0_attn_dt_w"], f32)[i][s0:s1, :].T)
            cw = np.asarray(inp["L0_attn_conv_w"], f32)[i][s0:s1, 0, :]
            m[f"l{i}_convw"] = cc(cw.reshape(E2, 128, KCV).transpose(1, 0, 2).reshape(128, E2 * KCV))
            m[f"l{i}_convb"] = cc(np.asarray(inp["L0_attn_conv_b"], f32)[i][s0:s1].reshape(E2, 128).T)
            m[f"l{i}_dtb"] = cc(np.asarray(inp["L0_attn_dt_b"], f32)[i][s0:s1].reshape(E2, 128).T)
            al = np.asarray(inp["L0_attn_A_log"], f32)[i][s0:s1, :]
            m[f"l{i}_alog"] = cc(al.reshape(E2, 128, NST).transpose(1, 0, 2).reshape(128, E2 * NST))
            m[f"l{i}_dp"] = cc(np.asarray(inp["L0_attn_D_p"], f32)[i][s0:s1].reshape(E2, 128).T)
            m[f"l{i}_normw"] = cc(np.asarray(inp["L0_attn_norm_w"], f32)[i].reshape(DT, 128).T)
        in_maps.append(m)
    res = run_bass_kernel_spmd(nc, in_maps, core_ids=list(range(8)))
    out = np.empty((B, L, D), dtype=f32)
    for c in range(8):
        b, q = c // 4, c % 4
        out[b, q * TLOC:(q + 1) * TLOC, :] = np.asarray(res.results[c]["out_dmaj"]).T
    return out


# version 15 (final)
# speedup vs baseline: 1.6512x; 1.0345x over previous
import numpy as np
import concourse.bass as bass
import concourse.mybir as mybir
from concourse.bass_utils import run_bass_kernel_spmd
from concourse.alu_op_type import AluOpType as OP

F32 = mybir.dt.float32
AF = mybir.ActivationFunctionType

B, L, D, ED, NST, KCV, RDT = 2, 8192, 1024, 2048, 16, 4, 64
TLOC = L // 4
ESL = ED // 8
BT = B * L
DT = D // 128
E2 = ESL // 128
TCA = 512
TCP = 1024
NCH = BT // TCP
TCF = 256
EPS = 1e-5

ENGS = ("y", "g", "v", "s", "t")


class Op:
    __slots__ = ("eng", "emit", "deps", "dma", "need_inc", "val", "acoll")

    def __init__(self, eng, emit, deps, dma, acoll=False):
        self.eng, self.emit, self.deps, self.dma = eng, emit, list(deps), dma
        self.acoll = acoll
        self.need_inc = dma or acoll
        self.val = None


class Plan:
    def __init__(self):
        self.ops = []
        self.last = {e: None for e in ENGS}
        self.bdeps = {e: [] for e in ENGS}

    def op(self, eng, emit, deps=(), dma=False, acoll=False):
        o = Op(eng, emit, [d for d in deps if d is not None], dma, acoll)
        if self.bdeps[eng]:
            o.deps.extend(self.bdeps[eng])
            self.bdeps[eng] = []
        self.ops.append(o)
        self.last[eng] = o
        return o

    def barrier(self):
        snap = [o for o in self.last.values() if o is not None]
        for e in ENGS:
            self.bdeps[e] = list(snap)

    def finalize(self):
        for o in self.ops:
            for d in o.deps:
                if d.eng != o.eng or d.dma or d.acoll:
                    d.need_inc = True
        ctr = {e: 0 for e in ENGS}
        for o in self.ops:
            if o.need_inc:
                ctr[o.eng] += 16 if o.dma else 1
                o.val = ctr[o.eng]
        self.totals = ctr


class PB:

    def __init__(self, ap):
        self.ap = ap
        self.readers = []
        self.writer = None


def build_nc():
    nc = bass.Bass("TRN2", target_bir_lowering=False)

    x_in = nc.dram_tensor("x_dmaj", [D, TLOC], F32, kind="ExternalInput").ap()
    id_in = nc.dram_tensor("ident_in", [128, 128], F32, kind="ExternalInput").ap()
    sel_in = nc.dram_tensor("sel_in", [128, 2 * NST * 128], F32, kind="ExternalInput").ap()
    out_d = nc.dram_tensor("out_dmaj", [D, TLOC], F32, kind="ExternalOutput").ap()
    lwin = []
    for i in range(2):
        lwin.append({k: nc.dram_tensor(f"l{i}_{k}", shp, F32, kind="ExternalInput").ap()
                     for k, shp in [("inwT", [D, 2 * ESL]), ("outwT", [ESL, D]),
                                    ("xpwT", [ESL, 96]), ("dtwT", [RDT, ESL]),
                                    ("convw", [128, E2 * KCV]), ("convb", [128, E2]),
                                    ("dtb", [128, E2]), ("alog", [128, E2 * NST]),
                                    ("dp", [128, E2]), ("normw", [128, DT])]})
    fwin = {nm: nc.dram_tensor(nm, shp, F32, kind="ExternalInput").ap()
            for nm, shp in [("cnw1", [D, D]), ("cnb1", [D, 1]),
                            ("cnw2", [D, D // 2]), ("cnb2", [D // 2, 1]),
                            ("cnw3", [D // 2, D // 4]), ("cnb3", [D // 4, 1]),
                            ("cnw4", [D // 4, 1]), ("cnb4", [1, 1]),
                            ("gcw1", [D, D // 2]), ("gcb1", [D // 2, 1]),
                            ("gcw2", [D // 2, D // 4]), ("gcb2", [D // 4, 1]),
                            ("gcw3", [D // 4, 1]), ("gcb3", [1, 1]), ("ar", [1, 1])]}
    xn_loc_h = [nc.dram_tensor(f"xn_loc{t}", [D, TCP], F32, kind="Internal").ap()
                for t in range(2)]
    xn_all_h = [nc.dram_tensor(f"xn_all{t}", [8 * D, TCP], F32, kind="Internal",
                               addr_space="Shared").ap() for t in range(2)]
    u_st = nc.dram_tensor("u_st", [ESL, BT], F32, kind="Internal").ap()
    zg_st = nc.dram_tensor("zg_st", [ESL, BT], F32, kind="Internal").ap()
    dbc_ih = [nc.dram_tensor(f"dbc_i{t}", [96, BT // 2], F32, kind="Internal").ap()
              for t in range(2)]
    dbc_oh = [nc.dram_tensor(f"dbc_o{t}", [96, BT // 2], F32, kind="Internal",
                             addr_space="Shared").ap() for t in range(2)]
    op_ih = [nc.dram_tensor(f"op_i{t}", [8 * D, TCP], F32, kind="Internal").ap()
             for t in range(2)]
    op_oh = [nc.dram_tensor(f"op_o{t}", [D, TCP], F32, kind="Internal").ap()
             for t in range(2)]
    gf_i = nc.dram_tensor("gf_i", [128, DT], F32, kind="Internal").ap()
    gf_o = nc.dram_tensor("gf_o", [128, DT], F32, kind="Internal").ap()
    xstash = nc.dram_tensor("xstash", [D, TLOC], F32, kind="Internal").ap()

    from contextlib import ExitStack
    with ExitStack() as stk:
        ec = stk.enter_context
        resid = ec(nc.sbuf_tensor("resid", [128, DT, TLOC], F32))
        smalls = ec(nc.sbuf_tensor("smalls", [128, 2048], F32))
        consts = ec(nc.sbuf_tensor("consts", [128, 512], F32))
        ident = ec(nc.sbuf_tensor("ident", [128, 128], F32))
        ones_t = ec(nc.sbuf_tensor("ones_t", [128, 128], F32))
        arena = ec(nc.sbuf_tensor("arena", [128, 29696], F32))
        pt_h = [ec(nc.psum_tensor(f"pt{i}", [128, 1024], F32)) for i in range(4)]
        ps_w = PB(pt_h[0][:, 0:512])
        ps_w2 = PB(pt_h[1][:, 0:512])
        ps_a = PB(pt_h[2][:, 0:512])
        ps_b = PB(pt_h[3][:, 0:512])
        tiles = [PB(pt_h[i][:, :]) for i in range(4)]
        sems = {e: ec(nc.semaphore(f"sem_{e}")) for e in ENGS}
        block = ec(nc.Block())

        p = Plan()

        c_off = [0]

        def calloc(n):
            o = c_off[0]
            c_off[0] += n
            assert c_off[0] <= 256
            return consts[:, o:o + n]

        pA = calloc(E2 * NST).rearrange("p (e n) -> p e n", e=E2)
        pA_raw = calloc(E2 * NST)
        dtb_raw = calloc(E2)
        convw_c = calloc(E2 * KCV).rearrange("p (e k) -> p e k", e=E2)
        convb_c = calloc(E2)
        dp_c = calloc(E2)
        normw_c = calloc(DT)
        ones_col = calloc(1)
        zeros_col = calloc(1)
        ones_row = ones_t[0:1, :]
        k0t = calloc(1)
        k1t = calloc(1)
        art = calloc(1)
        gft = calloc(1)
        carry_c = calloc(E2 * NST)
        _cmap = {}

        def constf(v):
            if v not in _cmap:
                _cmap[v] = calloc(1)
                _cinit.append(vmemset(_cmap[v], float(v)))
            return _cmap[v]
        _cinit = []

        class Ar:
            def __init__(self):
                self.off = 0

            def al(self, *shape):
                n = int(np.prod(shape))
                o = self.off
                self.off += n
                assert self.off <= 29696, f"arena overflow {self.off}"
                v = arena[:, o:o + n]
                if len(shape) == 2:
                    v = v.rearrange("p (a b) -> p a b", a=shape[0])
                return v

            def reset(self, off=0):
                self.off = off

        A = Ar()

        def dma(q, out, in_, deps=()):
            return p.op(q, lambda e, o=out, i=in_: e.dma_start(out=o, in_=i),
                        deps=deps, dma=True)

        def mm(bank, pairs, out_ap=None, deps=(), start=True, stop=True):
            tgt = out_ap if out_ap is not None else bank.ap[:]
            ops = []
            nmm = len(pairs)
            for j, (lh, rh) in enumerate(pairs):
                d = list(deps) + (bank.readers if (j == 0 and start) else []) \
                    + ([bank.writer] if (j == 0 and not start and bank.writer) else [])
                st = start and j == 0
                sp = stop and j == nmm - 1
                ops.append(p.op("t", lambda e, lh=lh, rh=rh, st=st, sp=sp, t=tgt:
                                e.matmul(t, lh, rh, start=st, stop=sp), deps=d))
            if start:
                bank.readers = []
            bank.writer = ops[-1]
            return ops[-1]

        def rd(bank, op):
            bank.readers.append(op)
            return op

        def act(out, in_, func, deps=(), src_bank=None, **kw):
            d = list(deps) + ([src_bank.writer] if src_bank else [])
            o = p.op("s", lambda e, o=out, i=in_, f=func, k=dict(kw):
                     e.activation(out=o, in_=i, func=f, **k), deps=d)
            if src_bank:
                rd(src_bank, o)
            return o

        def vtt(out, a, b, op_, deps=(), banks=()):
            d = list(deps) + [bk.writer for bk in banks if bk.writer]
            o = p.op("v", lambda e, o=out, x=a, y2=b, z=op_:
                     e.tensor_tensor(o, x, y2, z), deps=d)
            for bk in banks:
                rd(bk, o)
            return o

        def vts(out, a, s1, op_, deps=(), banks=()):
            d = list(deps) + [bk.writer for bk in banks if bk.writer]
            o = p.op("v", lambda e, o=out, x=a, s=s1, z=op_:
                     e.tensor_scalar(o, x, s, None, z), deps=d)
            for bk in banks:
                rd(bk, o)
            return o

        def vstt(out, in0, scal, in1, op0, op1, deps=(), banks=(), accum_out=None):
            d = list(deps) + [bk.writer for bk in banks if bk.writer]

            def f(e, o=out, i0=in0, sc=scal, i1=in1, a=op0, b2=op1, ac=accum_out):
                if ac is not None:
                    return e.scalar_tensor_tensor(o, i0, sc, i1, a, b2, accum_out=ac)
                return e.scalar_tensor_tensor(o, i0, sc, i1, a, b2)
            o = p.op("v", f, deps=d)
            for bk in banks:
                rd(bk, o)
            return o

        def vscan(out, d0, d1, init, deps=()):
            return p.op("v", lambda e, o=out, a=d0, b2=d1, i=init:
                        e.tensor_tensor_scan(o, a, b2, i, OP.mult, OP.add), deps=deps)

        def vcopy(out, in_, deps=()):
            return p.op("v", lambda e, o=out, i=in_: e.tensor_copy(o, i), deps=deps)

        def vred(out, in_, deps=()):
            return p.op("v", lambda e, o=out, i=in_:
                        e.tensor_reduce(out=o, in_=i, axis=mybir.AxisListType.X,
                                        op=OP.add), deps=deps)

        def vmemset(ap, val, deps=()):
            return p.op("v", lambda e, a=ap, v=val: e.memset(a, v), deps=deps)

        def coll(kind, op_, groups, in_ap, out_ap, deps=()):
            return p.op("g", lambda e, k=kind, o=op_, g=groups, i=in_ap, u2=out_ap:
                        e.collective_compute(k, o, replica_groups=g,
                                             ins=[i.opt()], outs=[u2.opt()]),
                        deps=deps, acoll=True)

        for dt in range(DT):
            dma("y", resid[:, dt, :], x_in[dt * 128:(dt + 1) * 128, :])
        dma("y", ident[:, :], id_in)
        vmemset(ones_col, 1.0)
        vmemset(ones_t[:, :], 1.0)
        vmemset(zeros_col, 0.0)
        dma("y", art[0:1, :], fwin["ar"])
        for _v in (EPS, -0.5, 1.0 / D):
            constf(_v)
        p.barrier()

        def ln_chunk(t0, tc, sqbuf, t1buf, out_buf, gf_ck=None, gfparts=None,
                     extra_dep_per_dt=None):
            sq_ops = [act(sqbuf[:, dt, 0:tc], resid[:, dt, t0:t0 + tc], AF.Square)
                      for dt in range(DT)]
            w_sx = mm(ps_w, [(ones_col, resid[:, dt, t0:t0 + tc]) for dt in range(DT)],
                      out_ap=ps_w.ap[0:1, 0:tc])
            w_sx2 = mm(ps_w2, [(ones_col, sqbuf[:, dt, 0:tc]) for dt in range(DT)],
                       out_ap=ps_w2.ap[0:1, 0:tc], deps=sq_ops)
            mu = smalls[0:1, 0:tc]
            ex2 = smalls[0:1, 512:512 + tc]
            var = smalls[0:1, 1024:1024 + tc]
            rstd = smalls[0:1, 1536:1536 + tc]
            e1 = vts(mu, ps_w.ap[0:1, 0:tc], 1.0 / D, OP.mult, banks=[ps_w])
            e2_ = vts(ex2, ps_w2.ap[0:1, 0:tc], 1.0 / D, OP.mult, banks=[ps_w2])
            e3 = vtt(var, mu, mu, OP.mult, deps=[e1])
            e4 = vtt(var, ex2, var, OP.subtract, deps=[e2_, e3])
            e5a = act(rstd, var, AF.Ln, bias=constf(EPS)[0:1, 0:1], deps=[e4])
            e5 = act(rstd, rstd, AF.Exp, scale=constf(-0.5)[0:1, 0:1], deps=[e5a])
            r1 = mm(ps_a, [(ones_row, mu)], out_ap=ps_a.ap[:, 0:tc],
                    deps=[e3])
            r2 = mm(ps_b, [(ones_row, rstd)], out_ap=ps_b.ap[:, 0:tc],
                    deps=[e5])
            outs = []
            for dt in range(DT):
                xd = [extra_dep_per_dt[dt]] if extra_dep_per_dt else []
                s1 = vtt(t1buf[:, dt, 0:tc], resid[:, dt, t0:t0 + tc], ps_a.ap[:, 0:tc],
                         OP.subtract, deps=[sq_ops[dt]] + xd, banks=[ps_a])
                ac = gfparts[:, dt, gf_ck:gf_ck + 1] if gf_ck is not None else None
                outs.append(vstt(out_buf[:, dt, 0:tc], t1buf[:, dt, 0:tc], 1.0,
                                 ps_b.ap[:, 0:tc], OP.bypass, OP.mult,
                                 deps=[s1], banks=[ps_b], accum_out=ac))
            return outs

        def ffn_phase():
            A.reset(0)
            w1 = A.al(DT, D)
            w2 = A.al(DT, D // 2)
            w3 = A.al(4, D // 4)
            w4 = A.al(2, 1)
            g1w = A.al(DT, D // 2)
            g2w = A.al(4, D // 4)
            g3w = A.al(2, 1)
            b1 = A.al(DT)
            b2 = A.al(4)
            b3 = A.al(2)
            b4 = A.al(1)
            gb1 = A.al(4)
            gb2 = A.al(2)
            gb3 = A.al(1)
            lnb = A.al(DT, TCF)
            h1 = A.al(DT, TCF)
            h2 = A.al(4, TCF)
            h3 = A.al(2, TCF)
            tmp8 = A.al(DT, TCF)
            t1b = A.al(DT, TCF)
            gfparts = A.al(DT, DT)
            gfp = A.al(DT)

            for src, dst in [(fwin["cnw1"], w1), (fwin["cnw2"], w2), (fwin["cnw3"], w3),
                             (fwin["cnw4"], w4), (fwin["gcw1"], g1w), (fwin["gcw2"], g2w),
                             (fwin["gcw3"], g3w)]:
                for k in range(src.shape[0] // 128):
                    dma("y", dst[:, k, :], src[k * 128:(k + 1) * 128, :])
            for src, dst in [(fwin["cnb1"], b1), (fwin["cnb2"], b2), (fwin["cnb3"], b3),
                             (fwin["gcb1"], gb1), (fwin["gcb2"], gb2)]:
                for k in range(src.shape[0] // 128):
                    dma("y", dst[:, k:k + 1], src[k * 128:(k + 1) * 128, :])
            dma("y", b4[0:1, 0:1], fwin["cnb4"])
            dma("y", gb3[0:1, 0:1], fwin["gcb3"])
            p.barrier()

            nchf = TLOC // TCF
            for ck in range(nchf):
                ln_chunk(ck * TCF, TCF, tmp8, t1b, lnb, gf_ck=ck, gfparts=gfparts)
                p.barrier()
            rg = vred(gfp[:, 0:DT], gfparts)
            d1 = dma("g", gf_i, gfp[:, 0:DT], deps=[rg])
            cc1 = coll("AllReduce", OP.add, [[0, 1, 2, 3], [4, 5, 6, 7]],
                       gf_i, gf_o, deps=[d1])
            d2 = dma("y", gfp[:, 0:DT], gf_o, deps=[cc1])
            mlv = vts(gfp[:, 0:DT], gfp[:, 0:DT], 1.0 / L, OP.mult, deps=[d2])

            gh1 = smalls[0:128, 0:4]
            gh2 = smalls[0:128, 4:6]
            lastb1 = []
            for m in range(4):
                wop = mm(ps_w, [(g1w[:, k, m * 128:(m + 1) * 128], gfp[:, k:k + 1])
                                for k in range(DT)],
                         out_ap=ps_w.ap[:, 0:1], deps=[mlv])
                lastb1.append(act(gh1[:, m:m + 1], ps_w.ap[:, 0:1], AF.Gelu,
                                  bias=gb1[:, m:m + 1], src_bank=ps_w))
            l2 = []
            for m in range(2):
                wop = mm(ps_w2, [(g2w[:, k, m * 128:(m + 1) * 128], gh1[:, k:k + 1])
                                 for k in range(4)],
                         out_ap=ps_w2.ap[:, 0:1], deps=lastb1)
                l2.append(act(gh2[:, m:m + 1], ps_w2.ap[:, 0:1], AF.Gelu,
                              bias=gb2[:, m:m + 1], src_bank=ps_w2))
            wop = mm(ps_w, [(g3w[:, k, 0:1], gh2[:, k:k + 1]) for k in range(2)],
                     out_ap=ps_w.ap[0:1, 0:1], deps=l2)
            gfs = act(gft[0:1, :], ps_w.ap[0:1, 0:1], AF.Sigmoid,
                      bias=gb3[0:1, 0:1], src_bank=ps_w)
            kk1 = vts(k1t[0:1, :], art[0:1, :], 0.6, OP.mult)
            t1_ = act(k0t[0:1, :], gft[0:1, :], AF.Copy, bias=-0.5, scale=0.7,
                      deps=[gfs])
            t2_ = vtt(k0t[0:1, :], k0t[0:1, :], art[0:1, :], OP.mult, deps=[t1_])
            t3_ = act(k0t[0:1, :], k0t[0:1, :], AF.Copy, bias=0.1, scale=2.0,
                      deps=[t2_])
            p.barrier()

            TW_SRC = [4, 5, 6, 7, 0, 1, 2, 3]
            TW_SGN = [+1, +1, -1, -1, -1, -1, +1, +1]
            for ck in range(nchf):
                t0 = ck * TCF
                lnb_w = ln_chunk(t0, TCF, tmp8, t1b, lnb)
                for cyc in range(3):
                    h1ops = []
                    for m in range(DT):
                        bk = ps_w if m % 2 == 0 else ps_w2
                        wop = mm(bk, [(w1[:, k, m * 128:(m + 1) * 128], lnb[:, k, :])
                                      for k in range(DT)], out_ap=bk.ap[:, 0:TCF],
                                 deps=lnb_w)
                        h1ops.append(act(h1[:, m, :], bk.ap[:, 0:TCF], AF.Gelu,
                                         bias=b1[:, m:m + 1], src_bank=bk))
                    h2ops = []
                    for m in range(4):
                        bk = ps_w if m % 2 == 0 else ps_w2
                        wop = mm(bk, [(w2[:, k, m * 128:(m + 1) * 128], h1[:, k, :])
                                      for k in range(DT)], out_ap=bk.ap[:, 0:TCF],
                                 deps=h1ops)
                        h2ops.append(act(h2[:, m, :], bk.ap[:, 0:TCF], AF.Gelu,
                                         bias=b2[:, m:m + 1], src_bank=bk))
                    h3ops = []
                    for m in range(2):
                        bk = ps_w if m % 2 == 0 else ps_w2
                        wop = mm(bk, [(w3[:, k, m * 128:(m + 1) * 128], h2[:, k, :])
                                      for k in range(4)], out_ap=bk.ap[:, 0:TCF],
                                 deps=h2ops)
                        h3ops.append(act(h3[:, m, :], bk.ap[:, 0:TCF], AF.Gelu,
                                         bias=b3[:, m:m + 1], src_bank=bk))
                    wop = mm(ps_w2, [(w4[:, k, :], h3[:, k, :]) for k in range(2)],
                             out_ap=ps_w2.ap[0:1, 0:TCF], deps=h3ops)
                    tf = smalls[0:1, 512:512 + TCF]
                    cpl = smalls[0:1, 1024:1024 + TCF]
                    a1 = act(tf, ps_w2.ap[0:1, 0:TCF], AF.Sigmoid,
                             bias=b4[0:1, 0:1], src_bank=ps_w2)
                    a2 = p.op("v", lambda e, o=cpl, i=tf: e.tensor_scalar(
                        o, i, k1t[0:1, 0:1], k0t[0:1, 0:1], OP.mult, OP.add),
                        deps=[a1, kk1, t3_])
                    rr = mm(ps_a, [(ones_row, cpl)],
                            out_ap=ps_a.ap[:, 0:TCF], deps=[a2])
                    tops = []
                    for dt in range(DT):
                        tops.append(vtt(tmp8[:, dt, :], lnb[:, TW_SRC[dt], :],
                                        ps_a.ap[:, 0:TCF], OP.mult,
                                        deps=lnb_w, banks=[ps_a]))
                    lnb_w = []
                    for dt in range(DT):
                        lnb_w.append(vtt(lnb[:, dt, :], lnb[:, dt, :], tmp8[:, dt, :],
                                         OP.add if TW_SGN[dt] > 0 else OP.subtract,
                                         deps=tops))
                for dt in range(DT):
                    vstt(resid[:, dt, t0:t0 + TCF], lnb[:, dt, :], 0.5,
                         resid[:, dt, t0:t0 + TCF], OP.mult, OP.add, deps=lnb_w)
                p.barrier()

        def attn_entry():
            A.reset(0)
            tmp = A.al(DT, TCA)
            sqt = A.al(DT, TCA)
            stash = [dma("g", xstash[dt * 128:(dt + 1) * 128, :], resid[:, dt, :])
                     for dt in range(DT)]
            for ck in range(TLOC // TCA):
                ln_chunk_into_resid(ck * TCA, TCA, sqt, tmp, stash)
                p.barrier()

        def ln_chunk_into_resid(t0, tc, sqbuf, t1buf, stash):
            sq_ops = [act(sqbuf[:, dt, 0:tc], resid[:, dt, t0:t0 + tc], AF.Square)
                      for dt in range(DT)]
            w_sx = mm(ps_w, [(ones_col, resid[:, dt, t0:t0 + tc]) for dt in range(DT)],
                      out_ap=ps_w.ap[0:1, 0:tc])
            w_sx2 = mm(ps_w2, [(ones_col, sqbuf[:, dt, 0:tc]) for dt in range(DT)],
                       out_ap=ps_w2.ap[0:1, 0:tc], deps=sq_ops)
            mu = smalls[0:1, 0:tc]
            ex2 = smalls[0:1, 512:512 + tc]
            var = smalls[0:1, 1024:1024 + tc]
            rstd = smalls[0:1, 1536:1536 + tc]
            e1 = vts(mu, ps_w.ap[0:1, 0:tc], 1.0 / D, OP.mult, banks=[ps_w])
            e2_ = vts(ex2, ps_w2.ap[0:1, 0:tc], 1.0 / D, OP.mult, banks=[ps_w2])
            e3 = vtt(var, mu, mu, OP.mult, deps=[e1])
            e4 = vtt(var, ex2, var, OP.subtract, deps=[e2_, e3])
            e5a = act(rstd, var, AF.Ln, bias=constf(EPS)[0:1, 0:1], deps=[e4])
            e5 = act(rstd, rstd, AF.Exp, scale=constf(-0.5)[0:1, 0:1], deps=[e5a])
            r1 = mm(ps_a, [(ones_row, mu)], out_ap=ps_a.ap[:, 0:tc], deps=[e3])
            r2 = mm(ps_b, [(ones_row, rstd)], out_ap=ps_b.ap[:, 0:tc], deps=[e5])
            outs = []
            for dt in range(DT):
                s1 = vtt(t1buf[:, dt, 0:tc], resid[:, dt, t0:t0 + tc], ps_a.ap[:, 0:tc],
                         OP.subtract, deps=[sq_ops[dt], w_sx] + stash, banks=[ps_a])
                outs.append(vstt(resid[:, dt, t0:t0 + tc], t1buf[:, dt, 0:tc], 1.0,
                                 ps_b.ap[:, 0:tc], OP.bypass, OP.mult,
                                 deps=[s1], banks=[ps_b]))
            return outs

        def mamba_layer(i):
            W = lwin[i]
            A.reset(0)
            selt = A.al(2 * NST, 128)
            inwT = A.al(DT, 2 * ESL)
            outwT = A.al(E2, D)
            xpwT = A.al(E2, 96)
            dtwT = A.al(ESL)
            wl = []
            for k in range(DT):
                wl.append(dma("y", inwT[:, k, :], W["inwT"][k * 128:(k + 1) * 128, :]))
            for k in range(E2):
                wl.append(dma("y", outwT[:, k, :], W["outwT"][k * 128:(k + 1) * 128, :]))
                wl.append(dma("y", xpwT[:, k, :], W["xpwT"][k * 128:(k + 1) * 128, :]))
            wl.append(dma("y", dtwT[0:RDT, :], W["dtwT"]))
            wl.append(dma("y", selt.rearrange("p a b -> p (a b)"), sel_in))
            wl.append(dma("y", convw_c.rearrange("p e k -> p (e k)"), W["convw"]))
            wl.append(dma("y", convb_c, W["convb"]))
            wl.append(dma("y", dtb_raw, W["dtb"]))
            wl.append(dma("y", pA_raw, W["alog"]))
            wl.append(dma("y", dp_c, W["dp"]))
            wl.append(dma("y", normw_c, W["normw"]))
            pa0 = act(pA.rearrange("p e n -> p (e n)"), pA_raw, AF.Exp, deps=wl)
            pa1 = vts(pA.rearrange("p e n -> p (e n)"), pA.rearrange("p e n -> p (e n)"),
                      -1.0, OP.mult, deps=[pa0])
            p.barrier()

            ub = A.al(E2, TCP)
            zgb = A.al(E2, TCP)
            xc = A.al(E2, TCP)
            dbcc = A.al(TCP)
            ckpt2 = A.off

            tmpa = A.al(DT, TCA)
            sqa = A.al(DT, TCA)
            for ck in range(TLOC // TCA):
                t0 = ck * TCA
                sq_ops = [act(sqa[:, dt, 0:TCA], resid[:, dt, t0:t0 + TCA], AF.Square)
                          for dt in range(DT)]
                w_sx2 = mm(ps_w, [(ones_col, sqa[:, dt, 0:TCA]) for dt in range(DT)],
                           out_ap=ps_w.ap[0:1, 0:TCA], deps=sq_ops)
                rstd = smalls[0:1, 0:TCA]
                e5a = act(rstd, ps_w.ap[0:1, 0:TCA], AF.Ln,
                          scale=constf(1.0 / D)[0:1, 0:1], bias=constf(EPS)[0:1, 0:1],
                          src_bank=ps_w)
                e5 = act(rstd, rstd, AF.Exp, scale=constf(-0.5)[0:1, 0:1], deps=[e5a])
                r2 = mm(ps_b, [(ones_row, rstd)], out_ap=ps_b.ap[:, 0:TCA],
                        deps=[e5])
                for dt in range(DT):
                    sop = vstt(tmpa[:, dt, 0:TCA], resid[:, dt, t0:t0 + TCA],
                               normw_c[:, dt:dt + 1], ps_b.ap[:, 0:TCA],
                               OP.mult, OP.mult, deps=[sq_ops[dt]], banks=[ps_b])
                    dma("g", xn_loc_h[ck // 2][dt * 128:(dt + 1) * 128,
                                               (ck % 2) * TCA:(ck % 2) * TCA + TCA],
                        tmpa[:, dt, 0:TCA], deps=[sop])
                p.barrier()
                if ck == 1:
                    ccg_h = [coll("AllGather", OP.bypass, [list(range(8))],
                                  xn_loc_h[0], xn_all_h[0], deps=[p.last["g"]])]
            ccg_h.append(coll("AllGather", OP.bypass, [list(range(8))],
                              xn_loc_h[1], xn_all_h[1], deps=[p.last["g"]]))
            p.op("g", lambda e: e.nop())
            p.barrier()

            A.reset(ckpt2)
            xn_in = A.al(DT, TCP)
            xi = A.al(E2, TCP + 3)
            ccr_h = []
            for k in range(NCH):
                rk = k // 2
                if k == NCH // 2:
                    ccr_h.append(coll("AllReduce", OP.add, [list(range(8))],
                                      dbc_ih[0], dbc_oh[0], deps=[p.last["g"]]))
                xl = [dma("y", xn_in[:, dt, :],
                          xn_all_h[k % 2][rk * D + dt * 128: rk * D + (dt + 1) * 128,
                                          0:TCP], deps=[ccg_h[k % 2]])
                      for dt in range(DT)]
                ev_xi = []
                ev_zg = []
                for m in range(4):
                    for tf in range(TCP // 512):
                        bk = ps_w if (m * 2 + tf) % 2 == 0 else ps_w2
                        wop = mm(bk, [(inwT[:, kk, m * 128:(m + 1) * 128],
                                       xn_in[:, kk, tf * 512:(tf + 1) * 512])
                                      for kk in range(DT)],
                                 out_ap=bk.ap[:, 0:512], deps=xl)
                        if m < 2:
                            ev_xi.append(act(xi[:, m, 3 + tf * 512: 3 + (tf + 1) * 512],
                                             bk.ap[:, 0:512], AF.Copy, src_bank=bk))
                        else:
                            ev_zg.append(act(zgb[:, m - 2, tf * 512:(tf + 1) * 512],
                                             bk.ap[:, 0:512], AF.Silu, src_bank=bk))
                halo = []
                if k % (NCH // 2) == 0:
                    halo = [vmemset(xi[:, :, 0:3], 0.0)]
                cops = []
                for e2 in range(E2):
                    c0 = vts(xc[:, e2, :], xi[:, e2, 0:TCP], convw_c[:, e2, 0:1],
                             OP.mult, deps=ev_xi + halo)
                    for kk in range(1, KCV):
                        c0 = vstt(xc[:, e2, :], xi[:, e2, kk:kk + TCP],
                                  convw_c[:, e2, kk:kk + 1], xc[:, e2, :],
                                  OP.mult, OP.add, deps=[c0])
                    cops.append(c0)
                uops = [act(ub[:, e2, :], xc[:, e2, :], AF.Silu,
                            bias=convb_c[:, e2:e2 + 1], deps=[cops[e2]])
                        for e2 in range(E2)]
                if k % (NCH // 2) != NCH // 2 - 1:
                    vcopy(xi[:, :, 0:3], xi[:, :, TCP:TCP + 3], deps=cops)
                dops = []
                for tf in range(TCP // 512):
                    wop = mm(ps_a, [(xpwT[:, e2, :], ub[:, e2, tf * 512:(tf + 1) * 512])
                                    for e2 in range(E2)],
                             out_ap=ps_a.ap[0:96, 0:512], deps=uops)
                    dops.append(act(dbcc[0:96, tf * 512:(tf + 1) * 512],
                                    ps_a.ap[0:96, 0:512], AF.Copy, src_bank=ps_a))
                dma("g", u_st.rearrange("(e pp) T -> pp e T", pp=128)[:, :, k * TCP:(k + 1) * TCP],
                    ub[:, :, :], deps=uops)
                dma("g", zg_st.rearrange("(e pp) T -> pp e T", pp=128)[:, :, k * TCP:(k + 1) * TCP],
                    zgb[:, :, :], deps=ev_zg)
                dma("g", dbc_ih[k // (NCH // 2)][:, (k % (NCH // 2)) * TCP:
                                                 (k % (NCH // 2) + 1) * TCP],
                    dbcc[0:96, 0:TCP], deps=dops)
                p.barrier()

            ccr_h.append(coll("AllReduce", OP.add, [list(range(8))],
                              dbc_ih[1], dbc_oh[1], deps=[p.last["g"]]))
            p.op("g", lambda e: e.nop())
            p.barrier()

            A.reset(ckpt2)
            lb = A.al(E2, TCP)
            lub = A.al(E2, TCP)
            bct = A.al(TCP)
            _op0 = A.off
            dab = [A.al(TCP), A.al(TCP)]
            dxb = A.al(TCP)
            hb = A.al(TCP)
            tmp_p = dxb
            g1b = hb
            yacc = [smalls[:, 0:TCP], smalls[:, TCP:2 * TCP]]
            ogb = A.al(E2, TCP)
            opst = arena[:, _op0:_op0 + DT * 512].rearrange("p (a b) -> p a b", a=DT)
            carry = carry_c.rearrange("p (e n) -> p e n", e=E2)
            ccs_h = []
            for k in range(NCH):
                rk, toff = k // 2, (k % 2) * TCP
                if k == NCH - 1:
                    ccs_h.append(coll("ReduceScatter", OP.add, [list(range(8))],
                                      op_ih[0], op_oh[0], deps=[p.last["g"]]))
                l_u = dma("y", ub[:, :, :],
                          u_st.rearrange("(e pp) T -> pp e T", pp=128)[:, :, k * TCP:(k + 1) * TCP])
                l_z = dma("y", zgb[:, :, :],
                          zg_st.rearrange("(e pp) T -> pp e T", pp=128)[:, :, k * TCP:(k + 1) * TCP])
                kh, kc = k // (NCH // 2), (k % (NCH // 2)) * TCP
                l_d = dma("y", dbcc[0:96, 0:TCP], dbc_oh[kh][:, kc:kc + TCP],
                          deps=[ccr_h[kh]])
                l_bc = dma("y", bct[0:32, 0:TCP], dbc_oh[kh][64:96, kc:kc + TCP],
                           deps=[ccr_h[kh]])
                ld4 = [l_u, l_z, l_d, l_bc]
                lops = []
                for e2 in range(E2):
                    for tf in range(TCP // 512):
                        bk = tiles[e2 * 2 + tf]
                        wop = mm(bk, [(dtwT[0:RDT, e2 * 128:(e2 + 1) * 128],
                                       dbcc[0:RDT, tf * 512:(tf + 1) * 512])],
                                 out_ap=bk.ap[:, 0:512], deps=ld4)
                        lops.append(act(lb[:, e2, tf * 512:(tf + 1) * 512],
                                        bk.ap[:, 0:512], AF.Abs,
                                        bias=dtb_raw[:, e2:e2 + 1], src_bank=bk))
                        lops.append(act(xc[:, e2, tf * 512:(tf + 1) * 512],
                                        bk.ap[:, 0:512], AF.Relu,
                                        bias=dtb_raw[:, e2:e2 + 1], src_bank=bk))
                ee = act(lb.rearrange("p e t -> p (e t)"),
                         lb.rearrange("p e t -> p (e t)"), AF.Exp, scale=-1.0,
                         deps=lops)
                ll = ee
                tmpd = arena[:, _op0:_op0 + TCP]
                ltd = arena[:, _op0 + 2 * TCP:_op0 + 3 * TCP]
                for e2 in range(E2):
                    t1 = vts(tmpd, lb[:, e2, :], 1.0, OP.add, deps=[ee])
                    t2 = act(ltd, tmpd, AF.Ln, deps=[t1])
                    t3 = vstt(tmpd, tmpd, -1.0, lb[:, e2, :], OP.mult, OP.add,
                              deps=[t2])
                    t4 = vts(tmpd, tmpd, 1.0, OP.add, deps=[t3])
                    t5 = vtt(xc[:, e2, :], xc[:, e2, :], ltd, OP.add, deps=[t2, t4])
                    ll = vtt(lb[:, e2, :], xc[:, e2, :], tmpd, OP.add, deps=[t5])
                lu_ = vtt(lub.rearrange("p e t -> p (e t)"),
                          lb.rearrange("p e t -> p (e t)"),
                          ub.rearrange("p e t -> p (e t)"), OP.mult,
                          deps=[ll] + ld4)
                gops = []
                last_scan = [None, None]
                last_add = [None, None]
                for n in range(NST):
                    tb, tc = tiles[n % 2], tiles[2 + n % 2]
                    for tf in range(TCP // 512):
                        bcs = bct[0:32, tf * 512:(tf + 1) * 512]
                        mm(tb, [(selt[0:32, n, :], bcs)],
                           out_ap=tb.ap[:, tf * 512:(tf + 1) * 512], deps=ld4)
                        mm(tc, [(selt[0:32, NST + n, :], bcs)],
                           out_ap=tc.ap[:, tf * 512:(tf + 1) * 512], deps=ld4)
                    for e2 in range(E2):
                        da_op = act(dab[e2][:, 0:TCP], lb[:, e2, :], AF.Exp,
                                    scale=pA[:, e2, n:n + 1],
                                    deps=[lu_, pa1, last_scan[e2]])
                        dx = vtt(dxb[:, 0:TCP], lub[:, e2, :], tb.ap[:, 0:TCP],
                                 OP.mult, deps=[lu_], banks=[tb])
                        init = zeros_col[:, 0:1] if k % (NCH // 2) == 0 \
                            else carry[:, e2, n:n + 1]
                        sc = vscan(hb[:, 0:TCP], dab[e2][:, 0:TCP], dxb[:, 0:TCP],
                                   init, deps=[da_op, dx])
                        last_scan[e2] = sc
                        tgt = yacc[e2] if n == 0 else tmp_p
                        pr0 = vtt(tgt[:, 0:512], hb[:, 0:512], tc.ap[:, 0:512],
                                  OP.mult, deps=[sc], banks=[tc])
                        vcopy(carry[:, e2, n:n + 1], hb[:, TCP - 1:TCP],
                              deps=[pr0])
                        pr1 = vtt(tgt[:, 512:TCP], hb[:, 512:TCP],
                                  tc.ap[:, 512:TCP], OP.mult, deps=[sc],
                                  banks=[tc])
                        if n == 0:
                            last_add[e2] = pr1
                        else:
                            last_add[e2] = vtt(yacc[e2][:, 0:TCP],
                                               yacc[e2][:, 0:TCP],
                                               tmp_p[:, 0:TCP],
                                               OP.add, deps=[pr1])
                for e2 in range(E2):
                    g1 = vstt(g1b[:, 0:TCP], ub[:, e2, :], dp_c[:, e2:e2 + 1],
                              yacc[e2][:, 0:TCP],
                              OP.mult, OP.add, deps=[last_add[e2]] + ld4)
                    gops.append(vtt(ogb[:, e2, :], g1b[:, 0:TCP],
                                    zgb[:, e2, :], OP.mult, deps=[g1] + ld4))
                for tf in range(TCP // 512):
                    evs = []
                    for m in range(DT):
                        bk = ps_w if m % 2 == 0 else ps_w2
                        wop = mm(bk, [(outwT[:, e2, m * 128:(m + 1) * 128],
                                       ogb[:, e2, tf * 512:(tf + 1) * 512])
                                      for e2 in range(E2)],
                                 out_ap=bk.ap[:, 0:512], deps=gops)
                        evs.append(act(opst[:, m, :], bk.ap[:, 0:512], AF.Copy,
                                       src_bank=bk))
                    for m in range(DT):
                        dma("g", op_ih[k % 2][rk * D + m * 128: rk * D + (m + 1) * 128,
                                             tf * 512:(tf + 1) * 512],
                            opst[:, m, :], deps=[evs[m]])
                p.barrier()

            ccs_h.append(coll("ReduceScatter", OP.add, [list(range(8))],
                              op_ih[1], op_oh[1], deps=[p.last["g"]]))
            p.op("g", lambda e: e.nop())
            p.barrier()
            A.reset(ckpt2)
            radd = A.al(DT, TCP)
            for ck in range(TLOC // TCP):
                t0 = ck * TCP
                l1 = [dma("y", radd[:, dt, :], op_oh[ck][dt * 128:(dt + 1) * 128, 0:TCP],
                          deps=[ccs_h[ck]]) for dt in range(DT)]
                for dt in range(DT):
                    vtt(resid[:, dt, t0:t0 + TCP], resid[:, dt, t0:t0 + TCP],
                        radd[:, dt, :], OP.add, deps=l1)
                p.barrier()

        ffn_phase()
        p.barrier()
        attn_entry()
        p.barrier()
        mamba_layer(0)
        p.barrier()
        mamba_layer(1)
        p.barrier()
        A.reset(0)
        tmp3 = A.al(DT, TCP)
        for ck in range(TLOC // TCP):
            t0 = ck * TCP
            l1 = [dma("y", tmp3[:, dt, :], xstash[dt * 128:(dt + 1) * 128, t0:t0 + TCP])
                  for dt in range(DT)]
            for dt in range(DT):
                vtt(resid[:, dt, t0:t0 + TCP], resid[:, dt, t0:t0 + TCP],
                    tmp3[:, dt, :], OP.add, deps=l1)
            p.barrier()
        ffn_phase()
        p.barrier()
        for dt in range(DT):
            dma("g", out_d[dt * 128:(dt + 1) * 128, :], resid[:, dt, :])

        p.finalize()
        hw = {e: {e2: 0 for e2 in ENGS} for e in ENGS}

        def emit_for(ename, eng):
            for o in p.ops:
                if o.eng != ename:
                    continue
                for d in o.deps:
                    if d.eng == ename and not (d.dma or d.acoll):
                        continue
                    if d.val is None:
                        continue
                    if d.val > hw[ename][d.eng]:
                        eng.wait_ge(sems[d.eng], d.val)
                        hw[ename][d.eng] = d.val
                inst = o.emit(eng)
                if o.need_inc:
                    inst.then_inc(sems[o.eng], 16 if o.dma else 1)
            if ename == "g":
                eng.wait_ge(sems["g"], p.totals["g"])

        @block.sync
        def _(sync):
            emit_for("y", sync)

        @block.gpsimd
        def _(gpsimd):
            emit_for("g", gpsimd)

        @block.vector
        def _(vector):
            emit_for("v", vector)

        @block.scalar
        def _(scalar):
            emit_for("s", scalar)

        @block.tensor
        def _(tensor):
            emit_for("t", tensor)

    return nc


_NC_CACHE = [None]


def kernel(**inp):
    f32 = np.float32
    cc = np.ascontiguousarray
    if _NC_CACHE[0] is None:
        _NC_CACHE[0] = build_nc()
    nc = _NC_CACHE[0]
    ident = np.eye(128, dtype=f32)
    sel = np.zeros((128, 2 * NST * 128), dtype=f32)
    for n in range(NST):
        sel[n, n * 128:(n + 1) * 128] = 1.0
        sel[NST + n, (NST + n) * 128:(NST + n + 1) * 128] = 1.0
    x = np.asarray(inp["L0_attn_x"], dtype=f32)
    in_maps = []
    for c in range(8):
        b, q = c // 4, c % 4
        s0, s1 = ESL * c, ESL * (c + 1)
        m = {
            "x_dmaj": cc(x[b, q * TLOC:(q + 1) * TLOC, :].T),
            "ident_in": ident,
            "sel_in": sel,
            "cnw1": cc(np.asarray(inp["L0_ffn_cn_w1"], f32)),
            "cnb1": cc(np.asarray(inp["L0_ffn_cn_b1"], f32)[:, None]),
            "cnw2": cc(np.asarray(inp["L0_ffn_cn_w2"], f32)),
            "cnb2": cc(np.asarray(inp["L0_ffn_cn_b2"], f32)[:, None]),
            "cnw3": cc(np.asarray(inp["L0_ffn_cn_w3"], f32)),
            "cnb3": cc(np.asarray(inp["L0_ffn_cn_b3"], f32)[:, None]),
            "cnw4": cc(np.asarray(inp["L0_ffn_cn_w4"], f32)),
            "cnb4": cc(np.asarray(inp["L0_ffn_cn_b4"], f32)[:, None]),
            "gcw1": cc(np.asarray(inp["L0_ffn_gc_w1"], f32)),
            "gcb1": cc(np.asarray(inp["L0_ffn_gc_b1"], f32)[:, None]),
            "gcw2": cc(np.asarray(inp["L0_ffn_gc_w2"], f32)),
            "gcb2": cc(np.asarray(inp["L0_ffn_gc_b2"], f32)[:, None]),
            "gcw3": cc(np.asarray(inp["L0_ffn_gc_w3"], f32)),
            "gcb3": cc(np.asarray(inp["L0_ffn_gc_b3"], f32)[:, None]),
            "ar": cc(np.asarray(inp["L0_ffn_adaptive_range"], f32)[:, None]),
        }
        for i in range(2):
            inw = np.asarray(inp["L0_attn_in_proj_w"], f32)[i]
            rows = np.concatenate([inw[s0:s1, :], inw[ED + s0:ED + s1, :]], axis=0)
            m[f"l{i}_inwT"] = cc(rows.T)
            m[f"l{i}_outwT"] = cc(np.asarray(inp["L0_attn_out_proj_w"], f32)[i][:, s0:s1].T)
            m[f"l{i}_xpwT"] = cc(np.asarray(inp["L0_attn_x_proj_w"], f32)[i][:, s0:s1].T)
            m[f"l{i}_dtwT"] = cc(np.asarray(inp["L0_attn_dt_w"], f32)[i][s0:s1, :].T)
            cw = np.asarray(inp["L0_attn_conv_w"], f32)[i][s0:s1, 0, :]
            m[f"l{i}_convw"] = cc(cw.reshape(E2, 128, KCV).transpose(1, 0, 2).reshape(128, E2 * KCV))
            m[f"l{i}_convb"] = cc(np.asarray(inp["L0_attn_conv_b"], f32)[i][s0:s1].reshape(E2, 128).T)
            m[f"l{i}_dtb"] = cc(np.asarray(inp["L0_attn_dt_b"], f32)[i][s0:s1].reshape(E2, 128).T)
            al = np.asarray(inp["L0_attn_A_log"], f32)[i][s0:s1, :]
            m[f"l{i}_alog"] = cc(al.reshape(E2, 128, NST).transpose(1, 0, 2).reshape(128, E2 * NST))
            m[f"l{i}_dp"] = cc(np.asarray(inp["L0_attn_D_p"], f32)[i][s0:s1].reshape(E2, 128).T)
            m[f"l{i}_normw"] = cc(np.asarray(inp["L0_attn_norm_w"], f32)[i].reshape(DT, 128).T)
        in_maps.append(m)
    res = run_bass_kernel_spmd(nc, in_maps, core_ids=list(range(8)))
    out = np.empty((B, L, D), dtype=f32)
    for c in range(8):
        b, q = c // 4, c % 4
        out[b, q * TLOC:(q + 1) * TLOC, :] = np.asarray(res.results[c]["out_dmaj"]).T
    return out
